# version 31
# baseline (speedup 1.0000x reference)
import math
import numpy as np
import ml_dtypes
import concourse.bass as bass
import concourse.mybir as mybir
from concourse.bass_utils import run_bass_kernel_spmd

F32 = mybir.dt.float32
BF16 = mybir.dt.bfloat16
AF = mybir.ActivationFunctionType
ALU = mybir.AluOpType
AX = mybir.AxisListType

T, C, H, D = 2048, 1024, 16, 64
NP = H // 2
TT = T // 128
CC = C // 128
TCH = T // 512
BF = ml_dtypes.bfloat16


class _Op:
    __slots__ = ("eng", "fn", "reads", "writes", "is_dma", "idx", "name",
                 "inc_needed", "inc_no", "waits", "resolved_waits", "batch")

    def __init__(self, eng, fn, reads, writes, is_dma, idx, name, batch=None):
        self.eng, self.fn = eng, fn
        self.reads, self.writes = tuple(reads), tuple(writes)
        self.is_dma, self.idx, self.name = is_dma, idx, name
        self.inc_needed = False
        self.inc_no = None
        self.waits = []
        self.resolved_waits = {}
        self.batch = batch


class _Sched:
    ENGINES = ("pe", "act", "dve", "sp", "sync")

    def __init__(self):
        self.ops = []
        self.last_writer = {}
        self.readers = {}

    def add(self, eng, fn, reads=(), writes=(), is_dma=False, name="", batch=None):
        idx = len(self.ops)
        op = _Op(eng, fn, reads, writes, is_dma, idx, name, batch=batch)
        deps = set()
        waw = set()
        for tok in op.reads:
            w = self.last_writer.get(tok)
            if w is not None:
                deps.add(w)
        for tok in op.writes:
            w = self.last_writer.get(tok)
            if w is not None:
                waw.add(w)
            for r in self.readers.get(tok, ()):
                deps.add(r)
        for tok in op.writes:
            self.last_writer[tok] = idx
            self.readers[tok] = []
        for tok in op.reads:
            self.readers.setdefault(tok, []).append(idx)
        for d in waw:
            dop = self.ops[d]
            if eng == "pe" and dop.eng == "pe" and not op.is_dma and not dop.is_dma:
                continue
            if (op.is_dma and dop.is_dma and batch is not None
                    and dop.batch == batch and dop.eng == eng):
                continue
            deps.add(d)
        op.waits = sorted(deps)
        self.ops.append(op)
        return idx

    @staticmethod
    def semkey(op):
        return op.eng + "q" if op.is_dma else op.eng

    def finalize(self):
        for op in self.ops:
            for d in op.waits:
                self.ops[d].inc_needed = True
        cum = {}
        for op in self.ops:
            k = self.semkey(op)
            if op.is_dma:
                cum[k] = cum.get(k, 0) + 16
                op.inc_no = cum[k]
            elif op.inc_needed:
                cum[k] = cum.get(k, 0) + 1
                op.inc_no = cum[k]
            else:
                op.inc_no = cum.get(k, 0)
        self.semkeys = sorted(cum)
        bmin, bmax = {}, {}
        for op in self.ops:
            if op.is_dma and op.batch is not None:
                k = (self.semkey(op), op.batch)
                bmin[k] = min(bmin.get(k, 1 << 30), op.inc_no)
                bmax[k] = max(bmax.get(k, 0), op.inc_no)

        def rep(dop):
            if dop.is_dma and dop.batch is not None:
                return bmax[(self.semkey(dop), dop.batch)]
            return dop.inc_no

        for op in self.ops:
            wv = {}
            for d in op.waits:
                dop = self.ops[d]
                k = self.semkey(dop)
                wv[k] = max(wv.get(k, 0), rep(dop))
            if op.is_dma:
                k = self.semkey(op)
                lo = bmin[(k, op.batch)] if op.batch is not None else op.inc_no
                if lo > 16:
                    wv[k] = max(wv.get(k, 0), lo - 16)
            op.resolved_waits = wv

    def make_bodies(self, sems):
        marks = {}
        per_eng = {e: [] for e in self.ENGINES}
        for op in self.ops:
            per_eng[op.eng].append(op)

        def mk(eng_name):
            def body(eng):
                for op in per_eng[eng_name]:
                    for pkey in sorted(op.resolved_waits):
                        val = op.resolved_waits[pkey]
                        if marks.get((eng_name, pkey), 0) >= val:
                            continue
                        marks[(eng_name, pkey)] = val
                        eng.wait_ge(sems[pkey], val)
                    ins = op.fn(eng)
                    if op.is_dma:
                        ins.then_inc(sems[self.semkey(op)], 16)
                    elif op.inc_needed:
                        ins.then_inc(sems[self.semkey(op)], 1)
            return body
        return {e: mk(e) for e in self.ENGINES}


def _rope_tables():
    inv = (1.0 / np.power(np.float32(10000.0),
                          np.arange(0, D, 2, dtype=np.float32) / np.float32(D)))
    pos = np.arange(T, dtype=np.float32)
    ang = pos[None, :] * inv[:, None]
    cos32 = np.cos(ang).astype(np.float32)
    sin32 = np.sin(ang).astype(np.float32)
    cos = np.tile(cos32, (4, 1))
    sgn = np.concatenate([-sin32, sin32], axis=0)
    sinp = np.tile(sgn, (2, 1))
    return cos.astype(BF), sinp.astype(BF)


def _build_nc(upto=99, taps=()):
    nc = bass.Bass("TRN2", target_bir_lowering=False)

    x_t = nc.dram_tensor("x", [T, C], F32, kind="ExternalInput")
    wq_t = nc.dram_tensor("wq", [C, C], F32, kind="ExternalInput")
    wk_t = nc.dram_tensor("wk", [C, C], F32, kind="ExternalInput")
    wv_t = nc.dram_tensor("wv", [C, C], F32, kind="ExternalInput")
    wo_t = nc.dram_tensor("wo", [C, C], F32, kind="ExternalInput")
    bo_t = nc.dram_tensor("bo", [C], F32, kind="ExternalInput")
    wrt_t = nc.dram_tensor("wrt", [C, C], F32, kind="ExternalInput")
    wkt_t = nc.dram_tensor("wkt", [C, C], F32, kind="ExternalInput")
    wvt_t = nc.dram_tensor("wvt", [C, C], F32, kind="ExternalInput")
    mur_t = nc.dram_tensor("mur", [C], F32, kind="ExternalInput")
    muk_t = nc.dram_tensor("muk", [C], F32, kind="ExternalInput")
    out_t = nc.dram_tensor("out", [T, C], F32, kind="ExternalOutput")

    cos_np, sin_np = _rope_tables()
    cos_t = nc.inline_tensor(np.ascontiguousarray(cos_np), name="cost")
    sin_t = nc.inline_tensor(np.ascontiguousarray(sin_np), name="sint")
    mneg_np = (np.where(np.arange(128)[:, None] > np.arange(128)[None, :],
                        np.float32(-30000.0), np.float32(0.0))).astype(BF)
    mneg_t = nc.inline_tensor(mneg_np, name="mnegt")
    id_t = nc.inline_tensor(np.eye(128, dtype=np.float32).astype(BF), name="idt")

    def sb(name, shape, dt, off):
        return nc.alloc_sbuf_tensor_at(name, shape, dt, offset=off)

    A0 = 20480
    qraw = [sb("qraw0", [128, 2048], BF16, 20480), sb("qraw1", [128, 2048], BF16, 24576)]
    swp = [sb("swp0", [128, 2048], BF16, 28672), sb("swp1", [128, 2048], BF16, 32768)]
    Er = [sb(f"er{k}", [128, 1024], BF16, 38912 + 2048 * k) for k in range(3)]
    cosb = sb("cosb", [128, 2048], BF16, 45056)
    sinb = sb("sinb", [128, 2048], BF16, 49152)
    WA = sb("wa", [128, 8192], BF16, 53248)
    WB = sb("wb", [128, 8192], BF16, 69632)
    Ax = sb("ax", [128, 16384], F32, A0)

    B_h1t = sb("h1t", [128, 16384], BF16, 86016)
    B_ao = sb("ao", [128, 16384], BF16, 86016)
    B_xr = sb("xr", [128, 16384], BF16, 86016)
    B_ka = sb("ka", [128, 16384], BF16, 86016)
    h2scr = [sb("h2s0", [128, 1024], BF16, 167936), sb("h2s1", [128, 1024], BF16, 169984),
             sb("h2s2", [128, 1024], BF16, 174080)]

    C_qt = sb("qt", [128, 16384], BF16, 118784)
    C_h2t = sb("h2t", [128, 16384], BF16, 118784)
    Wo_sb = sb("wosb", [128, 8192], BF16, 151552)
    bo32 = sb("bo32", [128, 1024], F32, 167936)
    boHi = sb("bohi", [128, 1024], BF16, 172032)
    boLo = sb("bolo", [128, 1024], BF16, 174080)
    WkT_sb = sb("wktsb", [128, 8192], BF16, 151552)
    WvT_sb = sb("wvtsb", [128, 8192], BF16, 135168)

    D_kt = sb("kt", [128, 16384], BF16, 151552)
    WrT_sb = sb("wrtsb", [128, 8192], BF16, 151552)
    xkr = [sb("xkr0", [128, 4096], BF16, 167936), sb("xkr1", [128, 4096], BF16, 176128)]
    xring = [sb(f"xri{i}", [128, 1024], F32, 167936 + 4096 * i) for i in range(4)]

    E_v = sb("vsb", [128, 16384], BF16, 184320)
    E_r = sb("rsb", [128, 16384], BF16, 184320)
    hscr = [sb(f"hs{k}", [128, 1024], BF16, 208896 + 2048 * k) for k in range(4)]

    ones128 = sb("ones128", [128, 128], BF16, 217088)
    mnegb = sb("mnegb", [128, 128], BF16, 217344)
    idb = sb("idb", [128, 128], BF16, 217600)
    murb = sb("murb", [128, 8], F32, 217856)
    mukb = sb("mukb", [128, 8], F32, 217888)
    stats = sb("stats", [128, 32], F32, 217920)
    DENl = sb("denl", [128, 1024], F32, 218048)
    qrawK = sb("qrawK", [128, 2048], BF16, 222144)
    aotmp = sb("aotmp", [128, 1024], BF16, 226240)
    dscr = sb("dscr", [128, 512], BF16, 228288)
    dscrD0 = sb("dscrd0", [128, 512], BF16, 172032)
    dscrD1 = sb("dscrd1", [128, 512], BF16, 173056)

    psA = nc.alloc_psum_tensor("psA", [128, 1024], F32)
    psB = nc.alloc_psum_tensor("psB", [128, 1024], F32)
    psO0 = nc.alloc_psum_tensor("psO0", [128, 1024], F32)
    psO1 = nc.alloc_psum_tensor("psO1", [128, 1024], F32)
    halfbanks = [(psA, 0), (psA, 512), (psB, 0), (psB, 512),
                 (psO0, 0), (psO0, 512), (psO1, 0), (psO1, 512)]

    s = _Sched()
    A = s.add

    def tap(nm, tns, rds, wrs=()):
        if nm not in taps:
            return
        dt_out = nc.dram_tensor("dbg_" + nm, list(tns.ap().shape),
                                tns.ap().dtype, kind="ExternalOutput")
        A("sync", (lambda dt_out=dt_out, tns=tns: lambda sy: sy.dma_start(
            dt_out.ap(), tns[:, :]))(), reads=list(rds), writes=list(wrs), is_dma=True,
          name=f"tap_{nm}")

    def wload(dst, dcol, src_t, srow, tok, after=()):
        def f(g):
            return g.dma_start(dst[:, dcol:dcol + 1024], src_t[srow:srow + 128, :])
        return ("sp", f, list(after), [tok])

    for cb in range(CC):
        e, f, rd, wr = wload(WA, 1024 * cb, wq_t, 128 * cb, f"waq{cb}")
        A("sp", f, reads=rd, writes=wr, is_dma=True, name=f"ld_wq{cb}", batch="wq")
    for cb in range(CC):
        e, f, rd, wr = wload(WB, 1024 * cb, wk_t, 128 * cb, f"wbk{cb}")
        A("sp", f, reads=rd, writes=wr, is_dma=True, name=f"ld_wk{cb}", batch="wk")

    A("sp", lambda g: g.dma_start(cosb[:, :], cos_t.ap()), writes=["cosb"], is_dma=True, name="ld_cos", batch="cst")
    A("sp", lambda g: g.dma_start(sinb[:, :], sin_t.ap()), writes=["sinb"], is_dma=True, name="ld_sin", batch="cst")
    A("sp", lambda g: g.dma_start(mnegb[:, :], mneg_t.ap()), writes=["mnegb"], is_dma=True, name="ld_mneg", batch="cst")
    A("sp", lambda g: g.dma_start(idb[:, :], id_t.ap()), writes=["idb"], is_dma=True, name="ld_id", batch="cst")

    def ld_mu(dst, src_t):
        def f(g):
            with nc.allow_non_contiguous_dma("tiny 4KB gather, one-time"):
                return g.dma_start(dst[:, 0:8], bass.AP(src_t, 0, [[1, 128], [128, 8]]))
        return f
    A("sp", ld_mu(murb, mur_t), writes=["murb"], is_dma=True, name="ld_mur", batch="cst")
    A("sp", ld_mu(mukb, muk_t), writes=["mukb"], is_dma=True, name="ld_muk", batch="cst")
    A("dve", lambda v: v.memset(ones128[:, :], 1.0), writes=["ones"], name="ones")

    def ln_stage1(src_ap_fn, hdst_fn, rd_tok, wr_tok, tag, extra_reads=(), sbase=0):
        st = lambda k: stats[:, sbase + k:sbase + k + 1]
        tk = lambda k: f"st{sbase + k}"
        A("dve", lambda v: v.reduce_sum(st(0), src_ap_fn(), axis=AX.X, negate=True),
          reads=[rd_tok], writes=[tk(0)], name=f"{tag}_red")
        A("dve", lambda v: v.tensor_scalar_mul(st(0), st(0), 1.0 / C),
          reads=[tk(0)], writes=[tk(0)], name=f"{tag}_nm")
        A("act", lambda sc: sc.activation(hdst_fn(), src_ap_fn(), AF.Square, accum_out=st(1)),
          reads=[rd_tok] + list(extra_reads), writes=[tk(1), wr_tok], name=f"{tag}_ssq")
        A("dve", lambda v: v.tensor_mul(st(3), st(0), st(0)),
          reads=[tk(0)], writes=[tk(3)], name=f"{tag}_m2")
        A("dve", lambda v: v.scalar_tensor_tensor(st(2), st(1), 1.0 / C, st(3),
                                                  ALU.mult, ALU.subtract),
          reads=[tk(1), tk(3)], writes=[tk(2)], name=f"{tag}_var")
        A("dve", lambda v: v.tensor_scalar_add(st(2), st(2), 1e-5),
          reads=[tk(2)], writes=[tk(2)], name=f"{tag}_eps")
        A("dve", lambda v: v.reciprocal(st(4), st(2)),
          reads=[tk(2)], writes=[tk(4)], name=f"{tag}_rv")

    def ln_stage2(src_ap_fn, hdst_fn, rd_tok, wr_tok, tag, extra_reads=(), sbase=0):
        st = lambda k: stats[:, sbase + k:sbase + k + 1]
        tk = lambda k: f"st{sbase + k}"
        A("act", lambda sc: sc.activation(st(5), st(4), AF.Sqrt),
          reads=[tk(4)], writes=[tk(5)], name=f"{tag}_rstd")
        A("dve", lambda v: v.tensor_mul(st(3), st(0), st(5)),
          reads=[tk(0), tk(5)], writes=[tk(3)], name=f"{tag}_nb")
        A("act", lambda sc: sc.activation(hdst_fn(), src_ap_fn(), AF.Identity,
                                          scale=st(5), bias=st(3)),
          reads=[rd_tok, tk(5), tk(3)], writes=[wr_tok], name=f"{tag}_norm")

    def ln_stats_and_h(src_ap_fn, hdst_fn, rd_tok, wr_tok, tag, extra_reads=(), sbase=0):
        ln_stage1(src_ap_fn, hdst_fn, rd_tok, wr_tok, tag, extra_reads, sbase)
        ln_stage2(src_ap_fn, hdst_fn, rd_tok, wr_tok, tag, extra_reads, sbase)

    def transpose_tile(h_ap_fn, dst, dst_col_base, psum, ps_toks, rd_toks, tag,
                       pbase=0):
        for half in range(2):
            for k4 in range(4):
                k = 4 * half + k4
                def tr(t, half=half, k4=k4, k=k):
                    o = psum[:, pbase + 512 * half + 64 * k4: pbase + 512 * half + 64 * k4 + 64].bitcast(BF16)
                    return t.matmul(o, h_ap_fn(k), idb[:, :],
                                    start=(k4 == 0), stop=(k4 == 3), is_transpose=True)
                A("pe", tr, reads=list(rd_toks) + ["idb"], writes=[ps_toks[half]],
                  name=f"{tag}_tr{k}")
            def ev(v, half=half):
                src = psum[:, pbase + 512 * half: pbase + 512 * half + 256].bitcast(BF16)
                o = bass.AP(dst, dst_col_base + 2048 * (4 * half),
                            [[16384, 128], [2048, 4], [1, 128]])
                return v.tensor_copy(o, src)
            A("dve", ev, reads=[ps_toks[half]],
              writes=[("h1t" if dst is B_h1t else "h2t"),
                      ("h1t" if dst is B_h1t else "h2t") + f"_tt{dst_col_base // 128}",
                      ps_toks[half]],
              name=f"{tag}_ev{half}")

    def tr_flush(j, which, dst, tag):
        psum, pbase, pst = (psO0 if j % 2 == 0 else psO1), 0, \
            (("ps4", "ps5") if j % 2 == 0 else ("ps6", "ps7"))
        hs = hscr if dst is B_h1t else h2scr
        transpose_tile((lambda: lambda k: hs[which][:, 128 * k:128 * (k + 1)])(),
                       dst, 128 * j, psum, pst,
                       [("hscr" if dst is B_h1t else "h2s") + str(which)], f"{tag}_{j}",
                       pbase=pbase)

    def emit_vproj(tiles):
        for tt in tiles:
            for half in range(2):
                slot = halfbanks[(tt * 2 + half) % 4]
                ptok = f"ps{(tt * 2 + half) % 4}"
                for cb in range(CC):
                    def mm(t, tt=tt, half=half, cb=cb, slot=slot):
                        ps, o = slot
                        return t.matmul(ps[:, o:o + 512],
                                        B_h1t[:, 2048 * cb + 128 * tt:2048 * cb + 128 * tt + 128],
                                        WA[:, 1024 * cb + 512 * half:1024 * cb + 512 * half + 512],
                                        start=(cb == 0), stop=(cb == CC - 1))
                    A("pe", mm, reads=[f"wav{cb}", f"h1t_tt{tt}"], writes=[ptok], name=f"vmm{tt}_{half}_{cb}")
                def ev(v, tt=tt, half=half, slot=slot):
                    ps, o = slot
                    return v.tensor_copy(E_v[:, 1024 * tt + 512 * half:1024 * tt + 512 * half + 512],
                                         ps[:, o:o + 512])
                war = [f"hscr{tt - 12}"] if tt >= 12 else []
                A("dve", ev, reads=[ptok], writes=["vsb", f"vsb{tt}", ptok] + war, name=f"vev{tt}_{half}")

    pend1 = []
    pend_s2 = []
    for j in range(TT):
        xr_ = xring[j % 4]
        A("sync",
          (lambda j=j, xr_=xr_: lambda g: g.dma_start(xr_[:, :], x_t[128 * j:128 * (j + 1), :]))(),
          writes=[f"xring{j % 4}"], is_dma=True, name=f"ldx{j}", batch=f"x{j // 2}")
        src = (lambda xr_=xr_: lambda: xr_[:, :])()
        hdst = (lambda j=j: lambda: hscr[j % 3][:, :])()
        args = (src, hdst, f"xring{j % 4}", f"hscr{j % 3}", f"ln1_{j}")
        ln_stage1(*args, sbase=8 * (j % 4))
        pend_s2.append((args, 8 * (j % 4)))
        if len(pend_s2) > 1:
            a, sb_ = pend_s2.pop(0)
            ln_stage2(*a, sbase=sb_)
        pend1.append((j, j % 3))
        if len(pend1) > 2:
            jj, w_ = pend1.pop(0)
            tr_flush(jj, w_, B_h1t, "t1")
    for a, sb_ in pend_s2:
        ln_stage2(*a, sbase=sb_)
    for jj, w_ in pend1:
        tr_flush(jj, w_, B_h1t, "t1")

    tap("h1t", B_h1t, ["h1t"])

    def proj_qk(dst, dst_tok, Wsb, wtoks, tag, cos=range(CC)):
        qring = [qrawK, qraw[1]] if tag == "k" else qraw
        qtoks = ["qrawK", "qraw1"] if tag == "k" else ["qraw0", "qraw1"]
        for co in cos:
            qr, qtok = qring[co % 2], qtoks[co % 2]
            sw, stok = swp[co % 2], f"swp{co % 2}"
            for tch in range(TCH):
                slot = halfbanks[(co * TCH + tch) % 8]
                ptok = f"ps{(co * TCH + tch) % 8}"
                for cb in range(CC):
                    def mm(t, co=co, tch=tch, cb=cb, slot=slot, Wsb=Wsb):
                        ps, o = slot
                        return t.matmul(ps[:, o:o + 512],
                                        Wsb[:, 1024 * cb + 128 * co:1024 * cb + 128 * co + 128],
                                        B_h1t[:, 2048 * cb + 512 * tch:2048 * cb + 512 * tch + 512],
                                        start=(cb == 0), stop=(cb == CC - 1))
                    A("pe", mm, reads=[wtoks[cb]] + [f"h1t_tt{t}" for t in range(4 * tch, 4 * tch + 4)],
                      writes=[ptok], name=f"{tag}mm{co}_{tch}_{cb}")
                def ev(sc, slot=slot, qr=qr, tch=tch):
                    ps, o = slot
                    return sc.copy(qr[:, 512 * tch:512 * (tch + 1)], ps[:, o:o + 512])
                A("act", ev, reads=[ptok], writes=[qtok, ptok], name=f"{tag}ev{co}_{tch}")
            for qi, (d0, s0) in enumerate([(0, 32), (32, 0), (64, 96), (96, 64)]):
                A("act" if qi % 2 == 0 else "sync",
                  (lambda d0=d0, s0=s0, sw=sw, qr=qr: lambda sc: sc.dma_start(
                    sw[d0:d0 + 32, :], qr[s0:s0 + 32, :]))(),
                  reads=[qtok], writes=[f"{stok}_{d0}"], is_dma=True, name=f"{tag}sw{co}_{d0}")
            stoks = [f"{stok}_0", f"{stok}_32", f"{stok}_64", f"{stok}_96"]
            A("dve", (lambda sw=sw: lambda v: v.tensor_mul(sw[:, :], sw[:, :], sinb[:, :]))(),
              reads=stoks + ["sinb"], writes=stoks, name=f"{tag}r1_{co}")
            A("dve", (lambda qr=qr: lambda v: v.tensor_mul(qr[:, :], qr[:, :], cosb[:, :]))(),
              reads=[qtok, "cosb"], writes=[qtok], name=f"{tag}r2_{co}")
            A("dve", (lambda co=co, dst=dst, qr=qr, sw=sw: lambda v: v.tensor_add(
                dst[:, 2048 * co:2048 * (co + 1)], qr[:, :], sw[:, :]))(),
              reads=[qtok] + stoks, writes=[dst_tok, f"{dst_tok}_p{co}"], name=f"{tag}r3_{co}")

    proj_qk(C_qt, "qt", WA, [f"waq{cb}" for cb in range(CC)], "q")
    for cb in range(CC):
        A("sp", (lambda cb=cb: lambda g: g.dma_start(WA[:, 1024 * cb:1024 * (cb + 1)],
                                                     wv_t[128 * cb:128 * (cb + 1), :]))(),
          reads=[], writes=[f"wav{cb}", f"waq{cb}"], is_dma=True, name=f"ld_wv{cb}", batch="wv")

    proj_qk(D_kt, "kt", WB, [f"wbk{cb}" for cb in range(CC)], "k")

    x_deps = {j: [] for j in range(TT)}
    for j in (0, 1):
        x_deps[j] = ["qraw0" if j == 0 else "qraw1"]
    for j in (2, 3):
        sw_ = "swp0" if j == 2 else "swp1"
        x_deps[j] = [f"{sw_}_{d0}" for d0 in (0, 32, 64, 96)]
    x_deps[4] = ["E0"]
    x_deps[5] = ["E1", "E2"]
    x_deps[6] = ["cosb"]
    x_deps[7] = ["sinb"]
    for j in range(8, 12):
        x_deps[j] = [f"waq{cb}" for cb in range(CC)]
    for j in range(12, 16):
        x_deps[j] = [f"wbk{cb}" for cb in range(CC)]

    emit_vproj(range(0, 4))

    tap("qt", C_qt, ["qt"], wrs=["h2t"])
    tap("kt", D_kt, ["kt"], wrs=["wo0"])
    tap("vsb", E_v, ["vsb"], wrs=[f"h2m{j}" for j in range(TT)])

    pair_order = [0, 1, 2, 3, 4, 5, 6, 7]
    unit_tail = [None]
    units = [(p, ic) for p in pair_order for ic in range(TCH)]
    steps = [(p, ic, J) for (p, ic) in units for J in range(4 * ic + 4)]

    def unit_ctx(p, ic):
            psO = psO0 if ic % 2 == 0 else psO1
            otokA = "ps4" if ic % 2 == 0 else "ps6"
            otokB = "ps5" if ic % 2 == 0 else "ps7"
            nJ = 4 * ic + 4

            def emit_scores(J, slot):
                i0 = max(512 * ic, 128 * J)
                w = 512 * (ic + 1) - i0
                sreg = psA if J % 2 == 0 else psB
                stokA = "ps0" if J % 2 == 0 else "ps2"
                stokB = "ps1" if J % 2 == 0 else "ps3"
                es, etok = Er[slot], f"E{slot}"
                diag = J >= 4 * ic
                A("pe", (lambda p=p, J=J, i0=i0, w=w, sreg=sreg, diag=diag: lambda t: t.matmul(
                    sreg[:, 0:w], D_kt[0:64, 2048 * p + 128 * J:2048 * p + 128 * J + 128],
                    C_qt[0:64, 2048 * p + i0:2048 * p + i0 + w], start=True, stop=not diag,
                    tile_position=(0, 0)))(),
                  reads=[f"kt_p{p}", f"qt_p{p}"], writes=[stokA, f"qt_r{p}", f"kt_r{p}"], name=f"sA{p}_{ic}_{J}")
                A("pe", (lambda p=p, J=J, i0=i0, w=w, sreg=sreg, diag=diag: lambda t: t.matmul(
                    sreg[:, 512:512 + w], D_kt[64:128, 2048 * p + 128 * J:2048 * p + 128 * J + 128],
                    C_qt[64:128, 2048 * p + i0:2048 * p + i0 + w], start=True, stop=not diag,
                    tile_position=(64, 0)))(),
                  reads=[f"kt_p{p}", f"qt_p{p}"], writes=[stokB, f"qt_r{p}", f"kt_r{p}"], name=f"sB{p}_{ic}_{J}")
                if diag:
                    A("pe", (lambda sreg=sreg: lambda t: t.matmul(
                        sreg[:, 0:128], idb[:, :], mnegb[:, :], start=False, stop=True))(),
                      reads=["idb", "mnegb"], writes=[stokA], name=f"mA{p}_{ic}_{J}")
                    A("pe", (lambda sreg=sreg: lambda t: t.matmul(
                        sreg[:, 512:640], idb[:, :], mnegb[:, :], start=False, stop=True))(),
                      reads=["idb", "mnegb"], writes=[stokB], name=f"mB{p}_{ic}_{J}")
                A("act", (lambda sreg=sreg, es=es, w=w: lambda sc: sc.activation(
                    es[:, :].rearrange("p (a b) -> p a b", a=2)[:, :, 0:w],
                    sreg[:, :].rearrange("p (a b) -> p a b", a=2)[:, :, 0:w],
                    AF.Exp, scale=0.125))(),
                  reads=[stokA, stokB], writes=[etok, stokA, stokB], name=f"ex{p}_{ic}_{J}")

            def emit_pv(J, slot):
                i0 = max(512 * ic, 128 * J)
                w = 512 * (ic + 1) - i0
                rel = i0 - 512 * ic
                es, etok = Er[slot], f"E{slot}"
                for hb, otok, ecol, ocol in ((0, otokA, 0, 0), (1, otokB, 512, 512)):
                    h = 2 * p + hb
                    A("pe", (lambda J=J, h=h, w=w, rel=rel, es=es, ecol=ecol, ocol=ocol, psO=psO, nJ=nJ:
                             lambda t: t.matmul(
                        psO[0:64, ocol + rel:ocol + rel + w],
                        E_v[:, 1024 * J + 64 * h:1024 * J + 64 * h + 64],
                        es[:, ecol:ecol + w],
                        start=(J == 0), stop=(J == nJ - 1), tile_position=(0, 0),
                        skip_group_check=True))(),
                      reads=[etok, f"vsb{J}"], writes=[otok], name=f"pv{p}_{ic}_{J}_{hb}")
                    A("pe", (lambda J=J, w=w, rel=rel, es=es, ecol=ecol, ocol=ocol, psO=psO, nJ=nJ:
                             lambda t: t.matmul(
                        psO[64:128, ocol + rel:ocol + rel + w],
                        ones128[:, 0:64],
                        es[:, ecol:ecol + w],
                        start=(J == 0), stop=(J == nJ - 1), tile_position=(0, 64),
                        skip_group_check=True))(),
                      reads=[etok, "ones"], writes=[otok], name=f"pd{p}_{ic}_{J}_{hb}")


            def tail_maker(p=p, ic=ic, psO=psO, otokA=otokA, otokB=otokB):
                A("dve", (lambda psO=psO: lambda v: v.reciprocal(
                    DENl[64:128, :], psO[64:128, :]))(),
                  reads=[otokA, otokB], writes=["denl"], name=f"rcp{p}_{ic}")
                A("dve", (lambda p=p, ic=ic, psO=psO: lambda v: v.tensor_mul(
                    B_ao[0:64, 2048 * p + 512 * ic:2048 * p + 512 * ic + 512],
                    psO[0:64, 0:512], DENl[64:128, 0:512]))(),
                  reads=[otokA, "denl", "h1t"], writes=["ao"], name=f"nmA{p}_{ic}")
                at_sl = slice(512 * (ic % 2), 512 * (ic % 2) + 512)
                A("dve", (lambda p=p, ic=ic, psO=psO, at_sl=at_sl: lambda v: v.tensor_mul(
                    aotmp[0:64, at_sl], psO[0:64, 512:1024], DENl[64:128, 512:1024]))(),
                  reads=[otokB, "denl"], writes=[f"aot{ic % 2}", otokB], name=f"nmB{p}_{ic}")
                A("sync", (lambda p=p, ic=ic, at_sl=at_sl: lambda sy: sy.dma_start(
                    B_ao[64:128, 2048 * p + 512 * ic:2048 * p + 512 * ic + 512],
                    aotmp[0:64, at_sl]))(),
                  reads=[f"aot{ic % 2}"], writes=["ao"], is_dma=True, name=f"amv{p}_{ic}")
            return emit_scores, emit_pv, tail_maker



    emitters = {(p, ic): unit_ctx(p, ic) for (p, ic) in units}
    cur_unit = [None]

    def do_scores(idx):
        p, ic, J = steps[idx]
        if (p, ic) != cur_unit[0]:
            if p == 0 and ic > 0:
                emit_vproj(range(4 * ic, 4 * ic + 4))
            cur_unit[0] = (p, ic)
        emitters[(p, ic)][0](J, idx % 3)

    do_scores(0)
    prev_unit = None
    for i, (p, ic, J) in enumerate(steps):
        if i + 1 < len(steps):
            do_scores(i + 1)
        if (p, ic) != prev_unit:
            if unit_tail[0] is not None:
                unit_tail[0](); unit_tail[0] = None
            prev_unit = (p, ic)
        emitters[(p, ic)][1](J, i % 3)
        if J == 4 * ic + 3:
            unit_tail[0] = emitters[(p, ic)][2]
    if unit_tail[0] is not None:
        unit_tail[0](); unit_tail[0] = None

    tap("ao", B_ao, ["ao"], wrs=[f"xr_t{t}" for t in range(4)])

    for cb in range(CC):
        A("sp", (lambda cb=cb: lambda g: g.dma_start(Wo_sb[:, 1024 * cb:1024 * (cb + 1)],
                                                     wo_t[128 * cb:128 * (cb + 1), :]))(),
          reads=["kt_r0", "kt_r1", "kt_r2", "kt_r3"],
          writes=[f"wo{cb}"], is_dma=True, name=f"ld_wo{cb}", batch="wo")

    for j in list(range(0, 4)) + list(range(6, 16)) + [4, 5]:
        A("sp", (lambda j=j: lambda g: g.dma_start(Ax[:, 1024 * j:1024 * (j + 1)],
                                                   x_t[128 * j:128 * (j + 1), :]))(),
          reads=[], writes=[f"xt{j}"] + x_deps[j], is_dma=True, name=f"ldx2_{j}",
          batch="xr2" if j not in (4, 5) else None)

    A("dve", lambda v: v.memset(bo32[0:64, :], 0.0), reads=["kt_r4", "kt_r5"], writes=["bo32"], name="bo_z")
    A("sp", lambda g: g.dma_start(bo32[0:1, :], bo_t.ap()),
      reads=["bo32"], writes=["bo32"], is_dma=True, name="bo_ld")
    A("dve", lambda v: v.tensor_copy(boHi[0:64, :], bo32[0:64, :]),
      reads=["bo32"], writes=["bohi"], name="bo_hi")
    A("dve", lambda v: v.tensor_sub(boLo[0:64, :], bo32[0:64, :], boHi[0:64, :]),
      reads=["bo32", "bohi"], writes=["bolo"], name="bo_lo")

    tt_order = [15, 14, 13, 12, 11, 10, 9, 8, 7, 6, 5, 4, 0, 3, 2, 1]
    pend2 = []
    pend2_s2 = []
    for n, tt in enumerate(tt_order):
        for half in range(2):
            slot = halfbanks[(n * 2 + half) % 4]
            ptok = f"ps{(n * 2 + half) % 4}"
            for cb in range(CC):
                A("pe", (lambda tt=tt, half=half, cb=cb, slot=slot: lambda t: t.matmul(
                    slot[0][:, slot[1]:slot[1] + 512],
                    B_ao[:, 2048 * cb + 128 * tt:2048 * cb + 128 * tt + 128],
                    Wo_sb[:, 1024 * cb + 512 * half:1024 * cb + 512 * half + 512],
                    start=(cb == 0), stop=False))(),
                  reads=["ao", f"wo{cb}"], writes=[ptok, "wodone"], name=f"wo{tt}_{half}_{cb}")
            A("pe", (lambda half=half, slot=slot: lambda t: t.matmul(
                slot[0][:, slot[1]:slot[1] + 512], ones128[0:1, 0:128],
                boHi[0:1, 512 * half:512 * half + 512], start=False, stop=False))(),
              reads=["bohi", "ones"], writes=[ptok], name=f"bh{tt}_{half}")
            A("pe", (lambda half=half, slot=slot: lambda t: t.matmul(
                slot[0][:, slot[1]:slot[1] + 512], ones128[0:1, 0:128],
                boLo[0:1, 512 * half:512 * half + 512], start=False, stop=True))(),
              reads=["bolo", "ones"], writes=[ptok], name=f"bl{tt}_{half}")
            A("dve", (lambda tt=tt, half=half, slot=slot: lambda v: v.tensor_add(
                Ax[:, 1024 * tt + 512 * half:1024 * tt + 512 * half + 512],
                Ax[:, 1024 * tt + 512 * half:1024 * tt + 512 * half + 512],
                slot[0][:, slot[1]:slot[1] + 512]))(),
              reads=[ptok, f"xt{tt}", f"res{tt}"], writes=[f"res{tt}", ptok], name=f"ra{tt}_{half}")
        lsrc = (lambda tt=tt: lambda: Ax[:, 1024 * tt:1024 * (tt + 1)])()
        lhdst = (lambda tt=tt: lambda: E_v[:, 1024 * tt:1024 * (tt + 1)])()
        args2 = (lsrc, lhdst, f"res{tt}", f"h2m{tt}", f"ln2_{tt}")
        ln_stage1(*args2, sbase=8 * (n % 4))
        pend2_s2.append((args2, 8 * (n % 4)))
        if len(pend2_s2) > 1:
            a, sb_ = pend2_s2.pop(0)
            ln_stage2(*a, sbase=sb_)
        pend2.append((tt, n))
        if len(pend2) > 2:
            jj, nn = pend2.pop(0)
            psum = psO0 if nn % 2 == 0 else psO1
            pst = ("ps4", "ps5") if nn % 2 == 0 else ("ps6", "ps7")
            transpose_tile((lambda jj=jj: lambda k: E_v[:, 1024 * jj + 128 * k:1024 * jj + 128 * (k + 1)])(),
                           C_h2t, 128 * jj, psum, pst,
                           [f"h2m{jj}"] + [f"qt_r{pp}" for pp in range(NP)], f"t2_{jj}")
    for a, sb_ in pend2_s2:
        ln_stage2(*a, sbase=sb_)
    for jj, nn in pend2:
        psum = psO0 if nn % 2 == 0 else psO1
        pst = ("ps4", "ps5") if nn % 2 == 0 else ("ps6", "ps7")
        transpose_tile((lambda jj=jj: lambda k: E_v[:, 1024 * jj + 128 * k:1024 * jj + 128 * (k + 1)])(),
                       C_h2t, 128 * jj, psum, pst,
                       [f"h2m{jj}"] + [f"qt_r{pp}" for pp in range(NP)], f"t2_{jj}")

    tap("xres1", Ax, [f"res{j}" for j in range(TT)])
    tap("h2t", C_h2t, ["h2t"] + [f"h2t_tt{j}" for j in range(TT)], wrs=["wkf0", "wvf0"])

    for cb in range(CC):
        A("sp",
          (lambda cb=cb: lambda g: g.dma_start(WrT_sb[:, 1024 * cb:1024 * (cb + 1)],
                                               wrt_t[128 * cb:128 * (cb + 1), :]))(),
          reads=["wodone"],
          writes=[f"wr{cb}"], is_dma=True, name=f"ld_wr{cb}", batch="wr")

    def mix(dst_of, dst_tok_of, mub, mu_tok, tag, extra_reads=(), dscrs=None, tchs=(3, 2, 1, 0)):
        for tch in tchs:
            need = [f"h2t_tt{t}" for t in range(max(0, 4 * tch - 1), 4 * tch + 4)]
            for cb in range(CC):
                base = 2048 * cb
                c0 = base + 512 * tch
                wd = 512 if tch > 0 else 511
                o0 = c0 if tch > 0 else c0 + 1
                dst, dcol = dst_of(cb, tch)
                d0 = dcol if tch > 0 else dcol + 1
                dtokw = dst_tok_of(tch)
                ds = dscrs[(tch * CC + cb) % len(dscrs)]
                dtok = f"{tag}ds{(tch * CC + cb) % len(dscrs)}"
                A("dve", (lambda o0=o0, wd=wd, ds=ds: lambda v: v.tensor_sub(
                    ds[:, 0:wd], C_h2t[:, o0:o0 + wd], C_h2t[:, o0 - 1:o0 - 1 + wd]))(),
                  reads=need + list(extra_reads), writes=[dtok], name=f"{tag}d{cb}_{tch}")
                A("dve", (lambda o0=o0, wd=wd, cb=cb, dst=dst, mub=mub, ds=ds, d0=d0: lambda v:
                          v.scalar_tensor_tensor(
                    dst[:, d0:d0 + wd], ds[:, 0:wd], mub[:, cb:cb + 1],
                    C_h2t[:, o0 - 1:o0 - 1 + wd], ALU.mult, ALU.add))(),
                  reads=[dtok] + need + [mu_tok] + list(extra_reads),
                  writes=[dtokw], name=f"{tag}m{cb}_{tch}")
                if tch == 0:
                    A("dve", (lambda base=base, cb=cb, dst=dst, mub=mub, dcol=dcol: lambda v: v.tensor_scalar(
                        dst[:, dcol:dcol + 1], C_h2t[:, base:base + 1], mub[:, cb:cb + 1], None, ALU.mult))(),
                      reads=need + [mu_tok], writes=[dtokw], name=f"{tag}c0_{cb}")

    mix(lambda cb, tch: (B_xr, 2048 * cb + 512 * tch),
        lambda tch: f"xr_t{tch}", murb, "murb", "xr", dscrs=[dscrD0, dscrD1],
        tchs=(3, 2))
    mix(lambda cb, tch_=3: (xkr[1], 512 * cb),
        lambda tch_: "xkr1", mukb, "mukb", "xk", dscrs=[dscr], tchs=(3,))
    mix(lambda cb, tch: (B_xr, 2048 * cb + 512 * tch),
        lambda tch: f"xr_t{tch}", murb, "murb", "xr", dscrs=[dscrD0, dscrD1],
        tchs=(1, 0))

    tap("xr", B_xr, [f"xr_t{t}" for t in range(4)], wrs=[f"ka_t{t}" for t in range(4)])

    for tt in [12, 13, 14, 15, 8, 9, 10, 11, 4, 5, 6, 7, 0, 1, 2, 3]:
        for half in range(2):
            slot = halfbanks[(tt * 2 + half) % 8]
            ptok = f"ps{(tt * 2 + half) % 8}"
            for cb in range(CC):
                A("pe", (lambda tt=tt, half=half, cb=cb, slot=slot: lambda t: t.matmul(
                    slot[0][:, slot[1]:slot[1] + 512],
                    B_xr[:, 2048 * cb + 128 * tt:2048 * cb + 128 * tt + 128],
                    WrT_sb[:, 1024 * cb + 512 * half:1024 * cb + 512 * half + 512],
                    start=(cb == 0), stop=(cb == CC - 1)))(),
                  reads=[f"xr_t{tt // 4}", f"wr{cb}"], writes=[ptok, "rmdone"], name=f"rm{tt}_{half}_{cb}")
            A("act", (lambda tt=tt, half=half, slot=slot: lambda sc: sc.activation(
                E_r[:, 1024 * tt + 512 * half:1024 * tt + 512 * half + 512],
                slot[0][:, slot[1]:slot[1] + 512], AF.Sigmoid))(),
              reads=[ptok, "vsb"], writes=[f"rs_t{tt}", ptok], name=f"sg{tt}_{half}")

    tap("rsb", E_r, [f"rs_t{t}" for t in range(TT)])

    for cb in range(CC):
        A("sp",
          (lambda cb=cb: lambda g: g.dma_start(WkT_sb[:, 1024 * cb:1024 * (cb + 1)],
                                               wkt_t[128 * cb:128 * (cb + 1), :]))(),
          reads=["rmdone"],
          writes=[f"wkf{cb}"], is_dma=True, name=f"ld_wkt{cb}", batch="wkt")

    def emit_km_chains(tch):
        out = []
        for dk in range(CC):
            slot = halfbanks[(tch * CC + dk) % 8]
            ptok = f"ps{(tch * CC + dk) % 8}"
            for cb in range(CC):
                A("pe", (lambda dk=dk, tch=tch, cb=cb, slot=slot: lambda t: t.matmul(
                    slot[0][:, slot[1]:slot[1] + 512],
                    WkT_sb[:, 1024 * cb + 128 * dk:1024 * cb + 128 * dk + 128],
                    xkr[tch % 2][:, 512 * cb:512 * cb + 512],
                    start=(cb == 0), stop=(cb == CC - 1)))(),
                  reads=[f"wkf{cb}", f"xkr{tch % 2}"], writes=[ptok, f"xkr_r{tch % 2}"],
                  name=f"km{dk}_{tch}_{cb}")
            out.append((dk, slot, ptok))
        return out

    def emit_km_evacs(tch, chains):
        for dk, slot, ptok in chains:
            A("act", (lambda dk=dk, tch=tch, slot=slot: lambda sc: sc.activation(
                B_ka[:, 2048 * dk + 512 * tch:2048 * dk + 512 * tch + 512],
                slot[0][:, slot[1]:slot[1] + 512], AF.Relu))(),
              reads=[ptok, "xr_t" + str(tch)], writes=[f"ka_t{tch}", ptok], name=f"kr{dk}_{tch}")
            A("dve", (lambda dk=dk, tch=tch: lambda v: v.tensor_mul(
                B_ka[:, 2048 * dk + 512 * tch:2048 * dk + 512 * tch + 512],
                B_ka[:, 2048 * dk + 512 * tch:2048 * dk + 512 * tch + 512],
                B_ka[:, 2048 * dk + 512 * tch:2048 * dk + 512 * tch + 512]))(),
              reads=[f"ka_t{tch}"], writes=[f"ka_t{tch}"], name=f"ks{dk}_{tch}")

    def emit_xkmix(tch):
        mix(lambda cb, tch_=tch: (xkr[tch_ % 2], 512 * cb),
            lambda tch_: f"xkr{tch_ % 2}", mukb, "mukb", "xk", dscrs=[dscr], tchs=(tch,))

    for tch in (3, 2, 1, 0):
        chains = emit_km_chains(tch)
        if tch > 0:
            emit_xkmix(tch - 1)
        emit_km_evacs(tch, chains)

    for cb in range(CC):
        A("sp",
          (lambda cb=cb: lambda g: g.dma_start(WvT_sb[:, 1024 * cb:1024 * (cb + 1)],
                                               wvt_t[128 * cb:128 * (cb + 1), :]))(),
          reads=["xkr0", "xkr1"] + [f"xr_t{t}" for t in range(4)],
          writes=[f"wvf{cb}"], is_dma=True, name=f"ld_wvt{cb}", batch="wvt")

    n_store = 0
    for tt in [12, 13, 14, 15, 8, 9, 10, 11, 4, 5, 6, 7, 0, 1, 2, 3]:
        for half in range(2):
            slot = halfbanks[(tt * 2 + half) % 8]
            ptok = f"ps{(tt * 2 + half) % 8}"
            for dk in range(CC):
                A("pe", (lambda tt=tt, half=half, dk=dk, slot=slot: lambda t: t.matmul(
                    slot[0][:, slot[1]:slot[1] + 512],
                    B_ka[:, 2048 * dk + 128 * tt:2048 * dk + 128 * tt + 128],
                    WvT_sb[:, 1024 * dk + 512 * half:1024 * dk + 512 * half + 512],
                    start=(dk == 0), stop=(dk == CC - 1)))(),
                  reads=[f"ka_t{tt // 4}", f"wvf{dk}"], writes=[ptok], name=f"vm{tt}_{half}_{dk}")
            A("dve", (lambda tt=tt, half=half, slot=slot: lambda v: v.tensor_mul(
                dscr[:, :], slot[0][:, slot[1]:slot[1] + 512],
                E_r[:, 1024 * tt + 512 * half:1024 * tt + 512 * half + 512]))(),
              reads=[ptok, f"rs_t{tt}"], writes=["p9scr", ptok], name=f"v1{tt}_{half}")
            A("dve", (lambda tt=tt, half=half: lambda v: v.tensor_add(
                Ax[:, 1024 * tt + 512 * half:1024 * tt + 512 * half + 512],
                Ax[:, 1024 * tt + 512 * half:1024 * tt + 512 * half + 512],
                dscr[:, :]))(),
              reads=["p9scr", f"res{tt}"], writes=[f"res{tt}"], name=f"v2{tt}_{half}")
        A("sync", (lambda tt=tt: lambda sy: sy.dma_start(
            out_t[128 * tt:128 * (tt + 1), :], Ax[:, 1024 * tt:1024 * (tt + 1)]))(),
          reads=[f"res{tt}"], writes=[], is_dma=True, name=f"st{tt}", batch="st")
        n_store += 1

    s.finalize()

    qtot = {}
    for op in s.ops:
        if op.is_dma:
            k = s.semkey(op)
            qtot[k] = max(qtot.get(k, 0), op.inc_no)
    import contextlib
    with contextlib.ExitStack() as stack:
        sems = {k: stack.enter_context(nc.semaphore("s_" + k)) for k in s.semkeys}
        blk = stack.enter_context(nc.Block())
        bodies = s.make_bodies(sems)
        blk.tensor(lambda t: bodies["pe"](t))
        blk.scalar(lambda sc: bodies["act"](sc))
        blk.vector(lambda v: bodies["dve"](v))
        blk.gpsimd(lambda g: bodies["sp"](g))

        def syb(sy):
            bodies["sync"](sy)
            for k in sorted(qtot):
                sy.wait_ge(sems[k], qtot[k])
        blk.sync(syb)
    return nc


_PERM = np.concatenate(
    [64 * h + np.concatenate([np.arange(0, 64, 2), np.arange(1, 64, 2)])
     for h in range(H)])


def kernel(L0_attn_x, L0_attn_Wq, L0_attn_Wk, L0_attn_Wv, L0_attn_Wo, L0_attn_bo,
           L0_ffn_W_r, L0_ffn_W_k, L0_ffn_W_v, L0_ffn_mu_r, L0_ffn_mu_k):
    x = np.ascontiguousarray(np.asarray(L0_attn_x, dtype=np.float32))
    wq = np.ascontiguousarray(np.asarray(L0_attn_Wq, dtype=np.float32)[:, _PERM])
    wk = np.ascontiguousarray(np.asarray(L0_attn_Wk, dtype=np.float32)[:, _PERM])
    wv = np.ascontiguousarray(np.asarray(L0_attn_Wv, dtype=np.float32))
    wo = np.ascontiguousarray(np.asarray(L0_attn_Wo, dtype=np.float32))
    bo = np.ascontiguousarray(np.asarray(L0_attn_bo, dtype=np.float32))
    wrt = np.ascontiguousarray(np.asarray(L0_ffn_W_r, dtype=np.float32).T)
    wkt = np.ascontiguousarray(np.asarray(L0_ffn_W_k, dtype=np.float32).T)
    wvt = np.ascontiguousarray(np.asarray(L0_ffn_W_v, dtype=np.float32).T)
    mur = np.ascontiguousarray(np.asarray(L0_ffn_mu_r, dtype=np.float32))
    muk = np.ascontiguousarray(np.asarray(L0_ffn_mu_k, dtype=np.float32))

    nc = _build_nc()
    in_maps = []
    for b in range(8):
        in_maps.append({"x": np.ascontiguousarray(x[b]), "wq": wq, "wk": wk,
                        "wv": wv, "wo": wo, "bo": bo, "wrt": wrt, "wkt": wkt,
                        "wvt": wvt, "mur": mur, "muk": muk})
    res = run_bass_kernel_spmd(nc, in_maps, core_ids=list(range(8)))
    return np.stack([r["out"] for r in res.results], axis=0)


if __name__ == "__main__":
    rng = np.random.default_rng(0)
    ins = {
        "L0_attn_x": rng.standard_normal((8, T, C), dtype=np.float32),
        "L0_attn_Wq": rng.standard_normal((C, C), dtype=np.float32) * 0.02,
        "L0_attn_Wk": rng.standard_normal((C, C), dtype=np.float32) * 0.02,
        "L0_attn_Wv": rng.standard_normal((C, C), dtype=np.float32) * 0.02,
        "L0_attn_Wo": rng.standard_normal((C, C), dtype=np.float32) * 0.02,
        "L0_attn_bo": rng.standard_normal(C, dtype=np.float32) * 0.1,
        "L0_ffn_W_r": rng.standard_normal((C, C), dtype=np.float32) * 0.03,
        "L0_ffn_W_k": rng.standard_normal((C, C), dtype=np.float32) * 0.03,
        "L0_ffn_W_v": rng.standard_normal((C, C), dtype=np.float32) * 0.03,
        "L0_ffn_mu_r": rng.random(C, dtype=np.float32),
        "L0_ffn_mu_k": rng.random(C, dtype=np.float32),
    }
    out = kernel(**ins)
    print("out", out.shape, out.dtype, float(np.abs(out).mean()))


# version 32
# speedup vs baseline: 1.0140x; 1.0140x over previous
import math
import numpy as np
import ml_dtypes
import concourse.bass as bass
import concourse.mybir as mybir
from concourse.bass_utils import run_bass_kernel_spmd

F32 = mybir.dt.float32
BF16 = mybir.dt.bfloat16
AF = mybir.ActivationFunctionType
ALU = mybir.AluOpType
AX = mybir.AxisListType

T, C, H, D = 2048, 1024, 16, 64
NP = H // 2
TT = T // 128
CC = C // 128
TCH = T // 512
BF = ml_dtypes.bfloat16


class _Op:
    __slots__ = ("eng", "fn", "reads", "writes", "is_dma", "idx", "name",
                 "inc_needed", "inc_no", "waits", "resolved_waits", "batch")

    def __init__(self, eng, fn, reads, writes, is_dma, idx, name, batch=None):
        self.eng, self.fn = eng, fn
        self.reads, self.writes = tuple(reads), tuple(writes)
        self.is_dma, self.idx, self.name = is_dma, idx, name
        self.inc_needed = False
        self.inc_no = None
        self.waits = []
        self.resolved_waits = {}
        self.batch = batch


class _Sched:
    ENGINES = ("pe", "act", "dve", "sp", "sync")

    def __init__(self):
        self.ops = []
        self.last_writer = {}
        self.readers = {}

    def add(self, eng, fn, reads=(), writes=(), is_dma=False, name="", batch=None):
        idx = len(self.ops)
        op = _Op(eng, fn, reads, writes, is_dma, idx, name, batch=batch)
        deps = set()
        waw = set()
        for tok in op.reads:
            w = self.last_writer.get(tok)
            if w is not None:
                deps.add(w)
        for tok in op.writes:
            w = self.last_writer.get(tok)
            if w is not None:
                waw.add(w)
            for r in self.readers.get(tok, ()):
                deps.add(r)
        for tok in op.writes:
            self.last_writer[tok] = idx
            self.readers[tok] = []
        for tok in op.reads:
            self.readers.setdefault(tok, []).append(idx)
        for d in waw:
            dop = self.ops[d]
            if eng == "pe" and dop.eng == "pe" and not op.is_dma and not dop.is_dma:
                continue
            if (op.is_dma and dop.is_dma and batch is not None
                    and dop.batch == batch and dop.eng == eng):
                continue
            deps.add(d)
        op.waits = sorted(deps)
        self.ops.append(op)
        return idx

    @staticmethod
    def semkey(op):
        return op.eng + "q" if op.is_dma else op.eng

    def finalize(self):
        for op in self.ops:
            for d in op.waits:
                self.ops[d].inc_needed = True
        cum = {}
        for op in self.ops:
            k = self.semkey(op)
            if op.is_dma:
                cum[k] = cum.get(k, 0) + 16
                op.inc_no = cum[k]
            elif op.inc_needed:
                cum[k] = cum.get(k, 0) + 1
                op.inc_no = cum[k]
            else:
                op.inc_no = cum.get(k, 0)
        self.semkeys = sorted(cum)
        bmin, bmax = {}, {}
        for op in self.ops:
            if op.is_dma and op.batch is not None:
                k = (self.semkey(op), op.batch)
                bmin[k] = min(bmin.get(k, 1 << 30), op.inc_no)
                bmax[k] = max(bmax.get(k, 0), op.inc_no)

        def rep(dop):
            if dop.is_dma and dop.batch is not None:
                return bmax[(self.semkey(dop), dop.batch)]
            return dop.inc_no

        for op in self.ops:
            wv = {}
            for d in op.waits:
                dop = self.ops[d]
                k = self.semkey(dop)
                wv[k] = max(wv.get(k, 0), rep(dop))
            if op.is_dma:
                k = self.semkey(op)
                lo = bmin[(k, op.batch)] if op.batch is not None else op.inc_no
                if lo > 16:
                    wv[k] = max(wv.get(k, 0), lo - 16)
            op.resolved_waits = wv

    def make_bodies(self, sems):
        marks = {}
        per_eng = {e: [] for e in self.ENGINES}
        for op in self.ops:
            per_eng[op.eng].append(op)

        def mk(eng_name):
            def body(eng):
                for op in per_eng[eng_name]:
                    for pkey in sorted(op.resolved_waits):
                        val = op.resolved_waits[pkey]
                        if marks.get((eng_name, pkey), 0) >= val:
                            continue
                        marks[(eng_name, pkey)] = val
                        eng.wait_ge(sems[pkey], val)
                    ins = op.fn(eng)
                    if op.is_dma:
                        ins.then_inc(sems[self.semkey(op)], 16)
                    elif op.inc_needed:
                        ins.then_inc(sems[self.semkey(op)], 1)
            return body
        return {e: mk(e) for e in self.ENGINES}


def _rope_tables():
    inv = (1.0 / np.power(np.float32(10000.0),
                          np.arange(0, D, 2, dtype=np.float32) / np.float32(D)))
    pos = np.arange(T, dtype=np.float32)
    ang = pos[None, :] * inv[:, None]
    cos32 = np.cos(ang).astype(np.float32)
    sin32 = np.sin(ang).astype(np.float32)
    cos = np.tile(cos32, (4, 1))
    sgn = np.concatenate([-sin32, sin32], axis=0)
    sinp = np.tile(sgn, (2, 1))
    return cos.astype(BF), sinp.astype(BF)


def _build_nc(upto=99, taps=()):
    nc = bass.Bass("TRN2", target_bir_lowering=False)

    x_t = nc.dram_tensor("x", [T, C], F32, kind="ExternalInput")
    wq_t = nc.dram_tensor("wq", [C, C], F32, kind="ExternalInput")
    wk_t = nc.dram_tensor("wk", [C, C], F32, kind="ExternalInput")
    wv_t = nc.dram_tensor("wv", [C, C], F32, kind="ExternalInput")
    wo_t = nc.dram_tensor("wo", [C, C], F32, kind="ExternalInput")
    bo_t = nc.dram_tensor("bo", [C], F32, kind="ExternalInput")
    wrt_t = nc.dram_tensor("wrt", [C, C], F32, kind="ExternalInput")
    wkt_t = nc.dram_tensor("wkt", [C, C], F32, kind="ExternalInput")
    wvt_t = nc.dram_tensor("wvt", [C, C], F32, kind="ExternalInput")
    mur_t = nc.dram_tensor("mur", [C], F32, kind="ExternalInput")
    muk_t = nc.dram_tensor("muk", [C], F32, kind="ExternalInput")
    out_t = nc.dram_tensor("out", [T, C], F32, kind="ExternalOutput")

    cos_np, sin_np = _rope_tables()
    cos_t = nc.inline_tensor(np.ascontiguousarray(cos_np), name="cost")
    sin_t = nc.inline_tensor(np.ascontiguousarray(sin_np), name="sint")
    mneg_np = (np.where(np.arange(128)[:, None] > np.arange(128)[None, :],
                        np.float32(-30000.0), np.float32(0.0))).astype(BF)
    mneg_t = nc.inline_tensor(mneg_np, name="mnegt")
    id_t = nc.inline_tensor(np.eye(128, dtype=np.float32).astype(BF), name="idt")

    def sb(name, shape, dt, off):
        return nc.alloc_sbuf_tensor_at(name, shape, dt, offset=off)

    A0 = 20480
    qraw = [sb("qraw0", [128, 2048], BF16, 20480), sb("qraw1", [128, 2048], BF16, 24576)]
    swp = [sb("swp0", [128, 2048], BF16, 28672), sb("swp1", [128, 2048], BF16, 32768)]
    Er = [sb(f"er{k}", [128, 1024], BF16, 38912 + 2048 * k) for k in range(3)]
    cosb = sb("cosb", [128, 2048], BF16, 45056)
    sinb = sb("sinb", [128, 2048], BF16, 49152)
    WA = sb("wa", [128, 8192], BF16, 53248)
    WB = sb("wb", [128, 8192], BF16, 69632)
    Ax = sb("ax", [128, 16384], F32, A0)

    B_h1t = sb("h1t", [128, 16384], BF16, 86016)
    B_ao = sb("ao", [128, 16384], BF16, 86016)
    B_xr = sb("xr", [128, 16384], BF16, 86016)
    B_ka = sb("ka", [128, 16384], BF16, 86016)
    h2scr = [sb("h2s0", [128, 1024], BF16, 167936), sb("h2s1", [128, 1024], BF16, 169984),
             sb("h2s2", [128, 1024], BF16, 174080)]

    C_qt = sb("qt", [128, 16384], BF16, 118784)
    C_h2t = sb("h2t", [128, 16384], BF16, 118784)
    Wo_sb = sb("wosb", [128, 8192], BF16, 151552)
    bo32 = sb("bo32", [128, 1024], F32, 167936)
    boHi = sb("bohi", [128, 1024], BF16, 172032)
    boLo = sb("bolo", [128, 1024], BF16, 174080)
    WkT_sb = sb("wktsb", [128, 8192], BF16, 151552)
    WvT_sb = sb("wvtsb", [128, 8192], BF16, 135168)

    D_kt = sb("kt", [128, 16384], BF16, 151552)
    WrT_sb = sb("wrtsb", [128, 8192], BF16, 151552)
    xkr = [sb("xkr0", [128, 4096], BF16, 167936), sb("xkr1", [128, 4096], BF16, 176128)]
    xring = [sb(f"xri{i}", [128, 1024], F32, 167936 + 4096 * i) for i in range(4)]

    E_v = sb("vsb", [128, 16384], BF16, 184320)
    E_r = sb("rsb", [128, 16384], BF16, 184320)
    hscr = [sb(f"hs{k}", [128, 1024], BF16, 208896 + 2048 * k) for k in range(4)]

    ones128 = sb("ones128", [128, 128], BF16, 217088)
    mnegb = sb("mnegb", [128, 128], BF16, 217344)
    idb = sb("idb", [128, 128], BF16, 217600)
    murb = sb("murb", [128, 8], F32, 217856)
    mukb = sb("mukb", [128, 8], F32, 217888)
    stats = sb("stats", [128, 32], F32, 217920)
    DENl = sb("denl", [128, 1024], F32, 218048)
    qrawK = sb("qrawK", [128, 2048], BF16, 222144)
    aotmp = sb("aotmp", [128, 1024], BF16, 226240)
    dscr = sb("dscr", [128, 512], BF16, 228288)
    dscrD0 = sb("dscrd0", [128, 512], BF16, 172032)
    dscrD1 = sb("dscrd1", [128, 512], BF16, 173056)

    psA = nc.alloc_psum_tensor("psA", [128, 1024], F32)
    psB = nc.alloc_psum_tensor("psB", [128, 1024], F32)
    psO0 = nc.alloc_psum_tensor("psO0", [128, 1024], F32)
    psO1 = nc.alloc_psum_tensor("psO1", [128, 1024], F32)
    halfbanks = [(psA, 0), (psA, 512), (psB, 0), (psB, 512),
                 (psO0, 0), (psO0, 512), (psO1, 0), (psO1, 512)]

    s = _Sched()
    A = s.add

    def tap(nm, tns, rds, wrs=()):
        if nm not in taps:
            return
        dt_out = nc.dram_tensor("dbg_" + nm, list(tns.ap().shape),
                                tns.ap().dtype, kind="ExternalOutput")
        A("sync", (lambda dt_out=dt_out, tns=tns: lambda sy: sy.dma_start(
            dt_out.ap(), tns[:, :]))(), reads=list(rds), writes=list(wrs), is_dma=True,
          name=f"tap_{nm}")

    A("sp", lambda g: g.dma_start(cosb[:, :], cos_t.ap()), writes=["cosb"], is_dma=True, name="ld_cos", batch="cst")
    A("sp", lambda g: g.dma_start(sinb[:, :], sin_t.ap()), writes=["sinb"], is_dma=True, name="ld_sin", batch="cst")
    A("sp", lambda g: g.dma_start(mnegb[:, :], mneg_t.ap()), writes=["mnegb"], is_dma=True, name="ld_mneg", batch="cst")
    A("sp", lambda g: g.dma_start(idb[:, :], id_t.ap()), writes=["idb"], is_dma=True, name="ld_id", batch="cst")

    def ld_mu(dst, src_t):
        def f(g):
            with nc.allow_non_contiguous_dma("tiny 4KB gather, one-time"):
                return g.dma_start(dst[:, 0:8], bass.AP(src_t, 0, [[1, 128], [128, 8]]))
        return f
    A("sp", ld_mu(murb, mur_t), writes=["murb"], is_dma=True, name="ld_mur", batch="cst")
    A("sp", ld_mu(mukb, muk_t), writes=["mukb"], is_dma=True, name="ld_muk", batch="cst")
    A("dve", lambda v: v.memset(ones128[:, :], 1.0), writes=["ones"], name="ones")

    def wload(dst, dcol, src_t, srow, tok, after=()):
        def f(g):
            return g.dma_start(dst[:, dcol:dcol + 1024], src_t[srow:srow + 128, :])
        return ("sp", f, list(after), [tok])

    for cb in range(CC):
        e, f, rd, wr = wload(WA, 1024 * cb, wq_t, 128 * cb, f"waq{cb}")
        A("sp", f, reads=rd, writes=wr, is_dma=True, name=f"ld_wq{cb}", batch="wq")
    for cb in range(CC):
        e, f, rd, wr = wload(WB, 1024 * cb, wk_t, 128 * cb, f"wbk{cb}")
        A("sp", f, reads=rd, writes=wr, is_dma=True, name=f"ld_wk{cb}", batch="wk")

    def ln_stage1(src_ap_fn, hdst_fn, rd_tok, wr_tok, tag, extra_reads=(), sbase=0):
        st = lambda k: stats[:, sbase + k:sbase + k + 1]
        tk = lambda k: f"st{sbase + k}"
        A("dve", lambda v: v.reduce_sum(st(0), src_ap_fn(), axis=AX.X, negate=True),
          reads=[rd_tok], writes=[tk(0)], name=f"{tag}_red")
        A("dve", lambda v: v.tensor_scalar_mul(st(0), st(0), 1.0 / C),
          reads=[tk(0)], writes=[tk(0)], name=f"{tag}_nm")
        A("act", lambda sc: sc.activation(hdst_fn(), src_ap_fn(), AF.Square, accum_out=st(1)),
          reads=[rd_tok] + list(extra_reads), writes=[tk(1), wr_tok], name=f"{tag}_ssq")
        A("dve", lambda v: v.tensor_mul(st(3), st(0), st(0)),
          reads=[tk(0)], writes=[tk(3)], name=f"{tag}_m2")
        A("dve", lambda v: v.scalar_tensor_tensor(st(2), st(1), 1.0 / C, st(3),
                                                  ALU.mult, ALU.subtract),
          reads=[tk(1), tk(3)], writes=[tk(2)], name=f"{tag}_var")
        A("dve", lambda v: v.tensor_scalar_add(st(2), st(2), 1e-5),
          reads=[tk(2)], writes=[tk(2)], name=f"{tag}_eps")
        A("dve", lambda v: v.reciprocal(st(4), st(2)),
          reads=[tk(2)], writes=[tk(4)], name=f"{tag}_rv")

    def ln_stage2(src_ap_fn, hdst_fn, rd_tok, wr_tok, tag, extra_reads=(), sbase=0):
        st = lambda k: stats[:, sbase + k:sbase + k + 1]
        tk = lambda k: f"st{sbase + k}"
        A("act", lambda sc: sc.activation(st(5), st(4), AF.Sqrt),
          reads=[tk(4)], writes=[tk(5)], name=f"{tag}_rstd")
        A("dve", lambda v: v.tensor_mul(st(3), st(0), st(5)),
          reads=[tk(0), tk(5)], writes=[tk(3)], name=f"{tag}_nb")
        A("act", lambda sc: sc.activation(hdst_fn(), src_ap_fn(), AF.Identity,
                                          scale=st(5), bias=st(3)),
          reads=[rd_tok, tk(5), tk(3)], writes=[wr_tok], name=f"{tag}_norm")

    def ln_stats_and_h(src_ap_fn, hdst_fn, rd_tok, wr_tok, tag, extra_reads=(), sbase=0):
        ln_stage1(src_ap_fn, hdst_fn, rd_tok, wr_tok, tag, extra_reads, sbase)
        ln_stage2(src_ap_fn, hdst_fn, rd_tok, wr_tok, tag, extra_reads, sbase)

    def transpose_tile(h_ap_fn, dst, dst_col_base, psum, ps_toks, rd_toks, tag,
                       pbase=0):
        for half in range(2):
            for k4 in range(4):
                k = 4 * half + k4
                def tr(t, half=half, k4=k4, k=k):
                    o = psum[:, pbase + 512 * half + 64 * k4: pbase + 512 * half + 64 * k4 + 64].bitcast(BF16)
                    return t.matmul(o, h_ap_fn(k), idb[:, :],
                                    start=(k4 == 0), stop=(k4 == 3), is_transpose=True)
                A("pe", tr, reads=list(rd_toks) + ["idb"], writes=[ps_toks[half]],
                  name=f"{tag}_tr{k}")
            def ev(v, half=half):
                src = psum[:, pbase + 512 * half: pbase + 512 * half + 256].bitcast(BF16)
                o = bass.AP(dst, dst_col_base + 2048 * (4 * half),
                            [[16384, 128], [2048, 4], [1, 128]])
                return v.tensor_copy(o, src)
            A("dve", ev, reads=[ps_toks[half]],
              writes=[("h1t" if dst is B_h1t else "h2t"),
                      ("h1t" if dst is B_h1t else "h2t") + f"_tt{dst_col_base // 128}",
                      ps_toks[half]],
              name=f"{tag}_ev{half}")

    def tr_flush(j, which, dst, tag):
        psum, pbase, pst = (psO0 if j % 2 == 0 else psO1), 0, \
            (("ps4", "ps5") if j % 2 == 0 else ("ps6", "ps7"))
        hs = hscr if dst is B_h1t else h2scr
        transpose_tile((lambda: lambda k: hs[which][:, 128 * k:128 * (k + 1)])(),
                       dst, 128 * j, psum, pst,
                       [("hscr" if dst is B_h1t else "h2s") + str(which)], f"{tag}_{j}",
                       pbase=pbase)

    def emit_vproj(tiles):
        for tt in tiles:
            for half in range(2):
                slot = halfbanks[(tt * 2 + half) % 4]
                ptok = f"ps{(tt * 2 + half) % 4}"
                for cb in range(CC):
                    def mm(t, tt=tt, half=half, cb=cb, slot=slot):
                        ps, o = slot
                        return t.matmul(ps[:, o:o + 512],
                                        B_h1t[:, 2048 * cb + 128 * tt:2048 * cb + 128 * tt + 128],
                                        WA[:, 1024 * cb + 512 * half:1024 * cb + 512 * half + 512],
                                        start=(cb == 0), stop=(cb == CC - 1))
                    A("pe", mm, reads=[f"wav{cb}", f"h1t_tt{tt}"], writes=[ptok], name=f"vmm{tt}_{half}_{cb}")
                def ev(v, tt=tt, half=half, slot=slot):
                    ps, o = slot
                    return v.tensor_copy(E_v[:, 1024 * tt + 512 * half:1024 * tt + 512 * half + 512],
                                         ps[:, o:o + 512])
                war = [f"hscr{tt - 12}"] if tt >= 12 else []
                A("dve", ev, reads=[ptok], writes=["vsb", f"vsb{tt}", ptok] + war, name=f"vev{tt}_{half}")

    pend1 = []
    pend_s2 = []
    for j in range(TT):
        xr_ = xring[j % 4]
        A("sync",
          (lambda j=j, xr_=xr_: lambda g: g.dma_start(xr_[:, :], x_t[128 * j:128 * (j + 1), :]))(),
          writes=[f"xring{j % 4}"], is_dma=True, name=f"ldx{j}", batch=f"x{j // 2}")
        src = (lambda xr_=xr_: lambda: xr_[:, :])()
        hdst = (lambda j=j: lambda: hscr[j % 3][:, :])()
        args = (src, hdst, f"xring{j % 4}", f"hscr{j % 3}", f"ln1_{j}")
        ln_stage1(*args, sbase=8 * (j % 4))
        pend_s2.append((args, 8 * (j % 4)))
        if len(pend_s2) > 1:
            a, sb_ = pend_s2.pop(0)
            ln_stage2(*a, sbase=sb_)
        pend1.append((j, j % 3))
        if len(pend1) > 2:
            jj, w_ = pend1.pop(0)
            tr_flush(jj, w_, B_h1t, "t1")
    for a, sb_ in pend_s2:
        ln_stage2(*a, sbase=sb_)
    for jj, w_ in pend1:
        tr_flush(jj, w_, B_h1t, "t1")

    tap("h1t", B_h1t, ["h1t"])

    def proj_qk(dst, dst_tok, Wsb, wtoks, tag, cos=range(CC)):
        qring = [qrawK, qraw[1]] if tag == "k" else qraw
        qtoks = ["qrawK", "qraw1"] if tag == "k" else ["qraw0", "qraw1"]
        for co in cos:
            qr, qtok = qring[co % 2], qtoks[co % 2]
            sw, stok = swp[co % 2], f"swp{co % 2}"
            for tch in range(TCH):
                slot = halfbanks[(co * TCH + tch) % 8]
                ptok = f"ps{(co * TCH + tch) % 8}"
                for cb in range(CC):
                    def mm(t, co=co, tch=tch, cb=cb, slot=slot, Wsb=Wsb):
                        ps, o = slot
                        return t.matmul(ps[:, o:o + 512],
                                        Wsb[:, 1024 * cb + 128 * co:1024 * cb + 128 * co + 128],
                                        B_h1t[:, 2048 * cb + 512 * tch:2048 * cb + 512 * tch + 512],
                                        start=(cb == 0), stop=(cb == CC - 1))
                    A("pe", mm, reads=[wtoks[cb]] + [f"h1t_tt{t}" for t in range(4 * tch, 4 * tch + 4)],
                      writes=[ptok], name=f"{tag}mm{co}_{tch}_{cb}")
                def ev(sc, slot=slot, qr=qr, tch=tch):
                    ps, o = slot
                    return sc.copy(qr[:, 512 * tch:512 * (tch + 1)], ps[:, o:o + 512])
                A("act", ev, reads=[ptok], writes=[qtok, ptok], name=f"{tag}ev{co}_{tch}")
            for qi, (d0, s0) in enumerate([(0, 32), (32, 0), (64, 96), (96, 64)]):
                A("act" if qi % 2 == 0 else "sync",
                  (lambda d0=d0, s0=s0, sw=sw, qr=qr: lambda sc: sc.dma_start(
                    sw[d0:d0 + 32, :], qr[s0:s0 + 32, :]))(),
                  reads=[qtok], writes=[f"{stok}_{d0}"], is_dma=True, name=f"{tag}sw{co}_{d0}")
            stoks = [f"{stok}_0", f"{stok}_32", f"{stok}_64", f"{stok}_96"]
            A("dve", (lambda sw=sw: lambda v: v.tensor_mul(sw[:, :], sw[:, :], sinb[:, :]))(),
              reads=stoks + ["sinb"], writes=stoks, name=f"{tag}r1_{co}")
            A("dve", (lambda qr=qr: lambda v: v.tensor_mul(qr[:, :], qr[:, :], cosb[:, :]))(),
              reads=[qtok, "cosb"], writes=[qtok], name=f"{tag}r2_{co}")
            A("dve", (lambda co=co, dst=dst, qr=qr, sw=sw: lambda v: v.tensor_add(
                dst[:, 2048 * co:2048 * (co + 1)], qr[:, :], sw[:, :]))(),
              reads=[qtok] + stoks, writes=[dst_tok, f"{dst_tok}_p{co}"], name=f"{tag}r3_{co}")

    proj_qk(C_qt, "qt", WA, [f"waq{cb}" for cb in range(CC)], "q")
    for cb in range(CC):
        A("sp", (lambda cb=cb: lambda g: g.dma_start(WA[:, 1024 * cb:1024 * (cb + 1)],
                                                     wv_t[128 * cb:128 * (cb + 1), :]))(),
          reads=[], writes=[f"wav{cb}", f"waq{cb}"], is_dma=True, name=f"ld_wv{cb}", batch="wv")

    proj_qk(D_kt, "kt", WB, [f"wbk{cb}" for cb in range(CC)], "k")

    x_deps = {j: [] for j in range(TT)}
    for j in (0, 1):
        x_deps[j] = ["qraw0" if j == 0 else "qraw1"]
    for j in (2, 3):
        sw_ = "swp0" if j == 2 else "swp1"
        x_deps[j] = [f"{sw_}_{d0}" for d0 in (0, 32, 64, 96)]
    x_deps[4] = ["E0"]
    x_deps[5] = ["E1", "E2"]
    x_deps[6] = ["cosb"]
    x_deps[7] = ["sinb"]
    for j in range(8, 12):
        x_deps[j] = [f"waq{cb}" for cb in range(CC)]
    for j in range(12, 16):
        x_deps[j] = [f"wbk{cb}" for cb in range(CC)]

    emit_vproj(range(0, 4))

    tap("qt", C_qt, ["qt"], wrs=["h2t"])
    tap("kt", D_kt, ["kt"], wrs=["wo0"])
    tap("vsb", E_v, ["vsb"], wrs=[f"h2m{j}" for j in range(TT)])

    pair_order = [0, 1, 2, 3, 4, 5, 6, 7]
    unit_tail = [None]
    units = [(p, ic) for p in pair_order for ic in range(TCH)]
    steps = [(p, ic, J) for (p, ic) in units for J in range(4 * ic + 4)]

    def unit_ctx(p, ic):
            psO = psO0 if ic % 2 == 0 else psO1
            otokA = "ps4" if ic % 2 == 0 else "ps6"
            otokB = "ps5" if ic % 2 == 0 else "ps7"
            nJ = 4 * ic + 4

            def emit_scores(J, slot):
                i0 = max(512 * ic, 128 * J)
                w = 512 * (ic + 1) - i0
                sreg = psA if J % 2 == 0 else psB
                stokA = "ps0" if J % 2 == 0 else "ps2"
                stokB = "ps1" if J % 2 == 0 else "ps3"
                es, etok = Er[slot], f"E{slot}"
                diag = J >= 4 * ic
                A("pe", (lambda p=p, J=J, i0=i0, w=w, sreg=sreg, diag=diag: lambda t: t.matmul(
                    sreg[:, 0:w], D_kt[0:64, 2048 * p + 128 * J:2048 * p + 128 * J + 128],
                    C_qt[0:64, 2048 * p + i0:2048 * p + i0 + w], start=True, stop=not diag,
                    tile_position=(0, 0)))(),
                  reads=[f"kt_p{p}", f"qt_p{p}"], writes=[stokA, f"qt_r{p}", f"kt_r{p}"], name=f"sA{p}_{ic}_{J}")
                A("pe", (lambda p=p, J=J, i0=i0, w=w, sreg=sreg, diag=diag: lambda t: t.matmul(
                    sreg[:, 512:512 + w], D_kt[64:128, 2048 * p + 128 * J:2048 * p + 128 * J + 128],
                    C_qt[64:128, 2048 * p + i0:2048 * p + i0 + w], start=True, stop=not diag,
                    tile_position=(64, 0)))(),
                  reads=[f"kt_p{p}", f"qt_p{p}"], writes=[stokB, f"qt_r{p}", f"kt_r{p}"], name=f"sB{p}_{ic}_{J}")
                if diag:
                    A("pe", (lambda sreg=sreg: lambda t: t.matmul(
                        sreg[:, 0:128], idb[:, :], mnegb[:, :], start=False, stop=True))(),
                      reads=["idb", "mnegb"], writes=[stokA], name=f"mA{p}_{ic}_{J}")
                    A("pe", (lambda sreg=sreg: lambda t: t.matmul(
                        sreg[:, 512:640], idb[:, :], mnegb[:, :], start=False, stop=True))(),
                      reads=["idb", "mnegb"], writes=[stokB], name=f"mB{p}_{ic}_{J}")
                A("act", (lambda sreg=sreg, es=es, w=w: lambda sc: sc.activation(
                    es[:, :].rearrange("p (a b) -> p a b", a=2)[:, :, 0:w],
                    sreg[:, :].rearrange("p (a b) -> p a b", a=2)[:, :, 0:w],
                    AF.Exp, scale=0.125))(),
                  reads=[stokA, stokB], writes=[etok, stokA, stokB], name=f"ex{p}_{ic}_{J}")

            def emit_pv(J, slot):
                i0 = max(512 * ic, 128 * J)
                w = 512 * (ic + 1) - i0
                rel = i0 - 512 * ic
                es, etok = Er[slot], f"E{slot}"
                for hb, otok, ecol, ocol in ((0, otokA, 0, 0), (1, otokB, 512, 512)):
                    h = 2 * p + hb
                    A("pe", (lambda J=J, h=h, w=w, rel=rel, es=es, ecol=ecol, ocol=ocol, psO=psO, nJ=nJ:
                             lambda t: t.matmul(
                        psO[0:64, ocol + rel:ocol + rel + w],
                        E_v[:, 1024 * J + 64 * h:1024 * J + 64 * h + 64],
                        es[:, ecol:ecol + w],
                        start=(J == 0), stop=(J == nJ - 1), tile_position=(0, 0),
                        skip_group_check=True))(),
                      reads=[etok, f"vsb{J}"], writes=[otok], name=f"pv{p}_{ic}_{J}_{hb}")
                    A("pe", (lambda J=J, w=w, rel=rel, es=es, ecol=ecol, ocol=ocol, psO=psO, nJ=nJ:
                             lambda t: t.matmul(
                        psO[64:128, ocol + rel:ocol + rel + w],
                        ones128[:, 0:64],
                        es[:, ecol:ecol + w],
                        start=(J == 0), stop=(J == nJ - 1), tile_position=(0, 64),
                        skip_group_check=True))(),
                      reads=[etok, "ones"], writes=[otok], name=f"pd{p}_{ic}_{J}_{hb}")


            def tail_maker(p=p, ic=ic, psO=psO, otokA=otokA, otokB=otokB):
                A("dve", (lambda psO=psO: lambda v: v.reciprocal(
                    DENl[64:128, :], psO[64:128, :]))(),
                  reads=[otokA, otokB], writes=["denl"], name=f"rcp{p}_{ic}")
                A("dve", (lambda p=p, ic=ic, psO=psO: lambda v: v.tensor_mul(
                    B_ao[0:64, 2048 * p + 512 * ic:2048 * p + 512 * ic + 512],
                    psO[0:64, 0:512], DENl[64:128, 0:512]))(),
                  reads=[otokA, "denl", "h1t"], writes=["ao"], name=f"nmA{p}_{ic}")
                at_sl = slice(512 * (ic % 2), 512 * (ic % 2) + 512)
                A("dve", (lambda p=p, ic=ic, psO=psO, at_sl=at_sl: lambda v: v.tensor_mul(
                    aotmp[0:64, at_sl], psO[0:64, 512:1024], DENl[64:128, 512:1024]))(),
                  reads=[otokB, "denl"], writes=[f"aot{ic % 2}", otokB], name=f"nmB{p}_{ic}")
                A("sync", (lambda p=p, ic=ic, at_sl=at_sl: lambda sy: sy.dma_start(
                    B_ao[64:128, 2048 * p + 512 * ic:2048 * p + 512 * ic + 512],
                    aotmp[0:64, at_sl]))(),
                  reads=[f"aot{ic % 2}"], writes=["ao"], is_dma=True, name=f"amv{p}_{ic}")
            return emit_scores, emit_pv, tail_maker



    emitters = {(p, ic): unit_ctx(p, ic) for (p, ic) in units}
    cur_unit = [None]

    def do_scores(idx):
        p, ic, J = steps[idx]
        if (p, ic) != cur_unit[0]:
            if p == 0 and ic > 0:
                emit_vproj(range(4 * ic, 4 * ic + 4))
            cur_unit[0] = (p, ic)
        emitters[(p, ic)][0](J, idx % 3)

    do_scores(0)
    prev_unit = None
    for i, (p, ic, J) in enumerate(steps):
        if i + 1 < len(steps):
            do_scores(i + 1)
        if (p, ic) != prev_unit:
            if unit_tail[0] is not None:
                unit_tail[0](); unit_tail[0] = None
            prev_unit = (p, ic)
        emitters[(p, ic)][1](J, i % 3)
        if J == 4 * ic + 3:
            unit_tail[0] = emitters[(p, ic)][2]
    if unit_tail[0] is not None:
        unit_tail[0](); unit_tail[0] = None

    tap("ao", B_ao, ["ao"], wrs=[f"xr_t{t}" for t in range(4)])

    for cb in range(CC):
        A("sp", (lambda cb=cb: lambda g: g.dma_start(Wo_sb[:, 1024 * cb:1024 * (cb + 1)],
                                                     wo_t[128 * cb:128 * (cb + 1), :]))(),
          reads=["kt_r0", "kt_r1", "kt_r2", "kt_r3"],
          writes=[f"wo{cb}"], is_dma=True, name=f"ld_wo{cb}", batch="wo")

    for j in list(range(0, 4)) + list(range(6, 16)) + [4, 5]:
        A("sp", (lambda j=j: lambda g: g.dma_start(Ax[:, 1024 * j:1024 * (j + 1)],
                                                   x_t[128 * j:128 * (j + 1), :]))(),
          reads=[], writes=[f"xt{j}"] + x_deps[j], is_dma=True, name=f"ldx2_{j}",
          batch="xr2" if j not in (4, 5) else None)

    A("dve", lambda v: v.memset(bo32[0:64, :], 0.0), reads=["kt_r4", "kt_r5"], writes=["bo32"], name="bo_z")
    A("sp", lambda g: g.dma_start(bo32[0:1, :], bo_t.ap()),
      reads=["bo32"], writes=["bo32"], is_dma=True, name="bo_ld")
    A("dve", lambda v: v.tensor_copy(boHi[0:64, :], bo32[0:64, :]),
      reads=["bo32"], writes=["bohi"], name="bo_hi")
    A("dve", lambda v: v.tensor_sub(boLo[0:64, :], bo32[0:64, :], boHi[0:64, :]),
      reads=["bo32", "bohi"], writes=["bolo"], name="bo_lo")

    tt_order = [15, 14, 13, 12, 11, 10, 9, 8, 7, 6, 5, 4, 0, 3, 2, 1]
    pend2 = []
    pend2_s2 = []
    for n, tt in enumerate(tt_order):
        for half in range(2):
            slot = halfbanks[(n * 2 + half) % 4]
            ptok = f"ps{(n * 2 + half) % 4}"
            for cb in range(CC):
                A("pe", (lambda tt=tt, half=half, cb=cb, slot=slot: lambda t: t.matmul(
                    slot[0][:, slot[1]:slot[1] + 512],
                    B_ao[:, 2048 * cb + 128 * tt:2048 * cb + 128 * tt + 128],
                    Wo_sb[:, 1024 * cb + 512 * half:1024 * cb + 512 * half + 512],
                    start=(cb == 0), stop=False))(),
                  reads=["ao", f"wo{cb}"], writes=[ptok, "wodone"], name=f"wo{tt}_{half}_{cb}")
            A("pe", (lambda half=half, slot=slot: lambda t: t.matmul(
                slot[0][:, slot[1]:slot[1] + 512], ones128[0:1, 0:128],
                boHi[0:1, 512 * half:512 * half + 512], start=False, stop=False))(),
              reads=["bohi", "ones"], writes=[ptok], name=f"bh{tt}_{half}")
            A("pe", (lambda half=half, slot=slot: lambda t: t.matmul(
                slot[0][:, slot[1]:slot[1] + 512], ones128[0:1, 0:128],
                boLo[0:1, 512 * half:512 * half + 512], start=False, stop=True))(),
              reads=["bolo", "ones"], writes=[ptok], name=f"bl{tt}_{half}")
            A("dve", (lambda tt=tt, half=half, slot=slot: lambda v: v.tensor_add(
                Ax[:, 1024 * tt + 512 * half:1024 * tt + 512 * half + 512],
                Ax[:, 1024 * tt + 512 * half:1024 * tt + 512 * half + 512],
                slot[0][:, slot[1]:slot[1] + 512]))(),
              reads=[ptok, f"xt{tt}", f"res{tt}"], writes=[f"res{tt}", ptok], name=f"ra{tt}_{half}")
        lsrc = (lambda tt=tt: lambda: Ax[:, 1024 * tt:1024 * (tt + 1)])()
        lhdst = (lambda tt=tt: lambda: E_v[:, 1024 * tt:1024 * (tt + 1)])()
        args2 = (lsrc, lhdst, f"res{tt}", f"h2m{tt}", f"ln2_{tt}")
        ln_stage1(*args2, sbase=8 * (n % 4))
        pend2_s2.append((args2, 8 * (n % 4)))
        if len(pend2_s2) > 1:
            a, sb_ = pend2_s2.pop(0)
            ln_stage2(*a, sbase=sb_)
        pend2.append((tt, n))
        if len(pend2) > 2:
            jj, nn = pend2.pop(0)
            psum = psO0 if nn % 2 == 0 else psO1
            pst = ("ps4", "ps5") if nn % 2 == 0 else ("ps6", "ps7")
            transpose_tile((lambda jj=jj: lambda k: E_v[:, 1024 * jj + 128 * k:1024 * jj + 128 * (k + 1)])(),
                           C_h2t, 128 * jj, psum, pst,
                           [f"h2m{jj}"] + [f"qt_r{pp}" for pp in range(NP)], f"t2_{jj}")
    for a, sb_ in pend2_s2:
        ln_stage2(*a, sbase=sb_)
    for jj, nn in pend2:
        psum = psO0 if nn % 2 == 0 else psO1
        pst = ("ps4", "ps5") if nn % 2 == 0 else ("ps6", "ps7")
        transpose_tile((lambda jj=jj: lambda k: E_v[:, 1024 * jj + 128 * k:1024 * jj + 128 * (k + 1)])(),
                       C_h2t, 128 * jj, psum, pst,
                       [f"h2m{jj}"] + [f"qt_r{pp}" for pp in range(NP)], f"t2_{jj}")

    tap("xres1", Ax, [f"res{j}" for j in range(TT)])
    tap("h2t", C_h2t, ["h2t"] + [f"h2t_tt{j}" for j in range(TT)], wrs=["wkf0", "wvf0"])

    for cb in range(CC):
        A("sp",
          (lambda cb=cb: lambda g: g.dma_start(WrT_sb[:, 1024 * cb:1024 * (cb + 1)],
                                               wrt_t[128 * cb:128 * (cb + 1), :]))(),
          reads=["wodone"],
          writes=[f"wr{cb}"], is_dma=True, name=f"ld_wr{cb}", batch="wr")

    def mix(dst_of, dst_tok_of, mub, mu_tok, tag, extra_reads=(), dscrs=None, tchs=(3, 2, 1, 0)):
        for tch in tchs:
            need = [f"h2t_tt{t}" for t in range(max(0, 4 * tch - 1), 4 * tch + 4)]
            for cb in range(CC):
                base = 2048 * cb
                c0 = base + 512 * tch
                wd = 512 if tch > 0 else 511
                o0 = c0 if tch > 0 else c0 + 1
                dst, dcol = dst_of(cb, tch)
                d0 = dcol if tch > 0 else dcol + 1
                dtokw = dst_tok_of(tch)
                ds = dscrs[(tch * CC + cb) % len(dscrs)]
                dtok = f"{tag}ds{(tch * CC + cb) % len(dscrs)}"
                A("dve", (lambda o0=o0, wd=wd, ds=ds: lambda v: v.tensor_sub(
                    ds[:, 0:wd], C_h2t[:, o0:o0 + wd], C_h2t[:, o0 - 1:o0 - 1 + wd]))(),
                  reads=need + list(extra_reads), writes=[dtok], name=f"{tag}d{cb}_{tch}")
                A("dve", (lambda o0=o0, wd=wd, cb=cb, dst=dst, mub=mub, ds=ds, d0=d0: lambda v:
                          v.scalar_tensor_tensor(
                    dst[:, d0:d0 + wd], ds[:, 0:wd], mub[:, cb:cb + 1],
                    C_h2t[:, o0 - 1:o0 - 1 + wd], ALU.mult, ALU.add))(),
                  reads=[dtok] + need + [mu_tok] + list(extra_reads),
                  writes=[dtokw], name=f"{tag}m{cb}_{tch}")
                if tch == 0:
                    A("dve", (lambda base=base, cb=cb, dst=dst, mub=mub, dcol=dcol: lambda v: v.tensor_scalar(
                        dst[:, dcol:dcol + 1], C_h2t[:, base:base + 1], mub[:, cb:cb + 1], None, ALU.mult))(),
                      reads=need + [mu_tok], writes=[dtokw], name=f"{tag}c0_{cb}")

    mix(lambda cb, tch: (B_xr, 2048 * cb + 512 * tch),
        lambda tch: f"xr_t{tch}", murb, "murb", "xr", dscrs=[dscrD0, dscrD1],
        tchs=(3, 2))
    mix(lambda cb, tch_=3: (xkr[1], 512 * cb),
        lambda tch_: "xkr1", mukb, "mukb", "xk", dscrs=[dscr], tchs=(3,))
    mix(lambda cb, tch: (B_xr, 2048 * cb + 512 * tch),
        lambda tch: f"xr_t{tch}", murb, "murb", "xr", dscrs=[dscrD0, dscrD1],
        tchs=(1, 0))

    tap("xr", B_xr, [f"xr_t{t}" for t in range(4)], wrs=[f"ka_t{t}" for t in range(4)])

    for tt in [12, 13, 14, 15, 8, 9, 10, 11, 4, 5, 6, 7, 0, 1, 2, 3]:
        for half in range(2):
            slot = halfbanks[(tt * 2 + half) % 8]
            ptok = f"ps{(tt * 2 + half) % 8}"
            for cb in range(CC):
                A("pe", (lambda tt=tt, half=half, cb=cb, slot=slot: lambda t: t.matmul(
                    slot[0][:, slot[1]:slot[1] + 512],
                    B_xr[:, 2048 * cb + 128 * tt:2048 * cb + 128 * tt + 128],
                    WrT_sb[:, 1024 * cb + 512 * half:1024 * cb + 512 * half + 512],
                    start=(cb == 0), stop=(cb == CC - 1)))(),
                  reads=[f"xr_t{tt // 4}", f"wr{cb}"], writes=[ptok, "rmdone"], name=f"rm{tt}_{half}_{cb}")
            A("act", (lambda tt=tt, half=half, slot=slot: lambda sc: sc.activation(
                E_r[:, 1024 * tt + 512 * half:1024 * tt + 512 * half + 512],
                slot[0][:, slot[1]:slot[1] + 512], AF.Sigmoid))(),
              reads=[ptok, "vsb"], writes=[f"rs_t{tt}", ptok], name=f"sg{tt}_{half}")

    tap("rsb", E_r, [f"rs_t{t}" for t in range(TT)])

    for cb in range(CC):
        A("sp",
          (lambda cb=cb: lambda g: g.dma_start(WkT_sb[:, 1024 * cb:1024 * (cb + 1)],
                                               wkt_t[128 * cb:128 * (cb + 1), :]))(),
          reads=["rmdone"],
          writes=[f"wkf{cb}"], is_dma=True, name=f"ld_wkt{cb}", batch="wkt")

    def emit_km_chains(tch):
        out = []
        for dk in range(CC):
            slot = halfbanks[(tch * CC + dk) % 8]
            ptok = f"ps{(tch * CC + dk) % 8}"
            for cb in range(CC):
                A("pe", (lambda dk=dk, tch=tch, cb=cb, slot=slot: lambda t: t.matmul(
                    slot[0][:, slot[1]:slot[1] + 512],
                    WkT_sb[:, 1024 * cb + 128 * dk:1024 * cb + 128 * dk + 128],
                    xkr[tch % 2][:, 512 * cb:512 * cb + 512],
                    start=(cb == 0), stop=(cb == CC - 1)))(),
                  reads=[f"wkf{cb}", f"xkr{tch % 2}"], writes=[ptok, f"xkr_r{tch % 2}"],
                  name=f"km{dk}_{tch}_{cb}")
            out.append((dk, slot, ptok))
        return out

    def emit_km_evacs(tch, chains):
        for dk, slot, ptok in chains:
            A("act", (lambda dk=dk, tch=tch, slot=slot: lambda sc: sc.activation(
                B_ka[:, 2048 * dk + 512 * tch:2048 * dk + 512 * tch + 512],
                slot[0][:, slot[1]:slot[1] + 512], AF.Relu))(),
              reads=[ptok, "xr_t" + str(tch)], writes=[f"ka_t{tch}", ptok], name=f"kr{dk}_{tch}")
            A("dve", (lambda dk=dk, tch=tch: lambda v: v.tensor_mul(
                B_ka[:, 2048 * dk + 512 * tch:2048 * dk + 512 * tch + 512],
                B_ka[:, 2048 * dk + 512 * tch:2048 * dk + 512 * tch + 512],
                B_ka[:, 2048 * dk + 512 * tch:2048 * dk + 512 * tch + 512]))(),
              reads=[f"ka_t{tch}"], writes=[f"ka_t{tch}"], name=f"ks{dk}_{tch}")

    def emit_xkmix(tch):
        mix(lambda cb, tch_=tch: (xkr[tch_ % 2], 512 * cb),
            lambda tch_: f"xkr{tch_ % 2}", mukb, "mukb", "xk", dscrs=[dscr], tchs=(tch,))

    for tch in (3, 2, 1, 0):
        chains = emit_km_chains(tch)
        if tch > 0:
            emit_xkmix(tch - 1)
        emit_km_evacs(tch, chains)

    for cb in range(CC):
        A("sp",
          (lambda cb=cb: lambda g: g.dma_start(WvT_sb[:, 1024 * cb:1024 * (cb + 1)],
                                               wvt_t[128 * cb:128 * (cb + 1), :]))(),
          reads=["xkr0", "xkr1"] + [f"xr_t{t}" for t in range(4)],
          writes=[f"wvf{cb}"], is_dma=True, name=f"ld_wvt{cb}", batch="wvt")

    n_store = 0
    for tt in [12, 13, 14, 15, 8, 9, 10, 11, 4, 5, 6, 7, 0, 1, 2, 3]:
        for half in range(2):
            slot = halfbanks[(tt * 2 + half) % 8]
            ptok = f"ps{(tt * 2 + half) % 8}"
            for dk in range(CC):
                A("pe", (lambda tt=tt, half=half, dk=dk, slot=slot: lambda t: t.matmul(
                    slot[0][:, slot[1]:slot[1] + 512],
                    B_ka[:, 2048 * dk + 128 * tt:2048 * dk + 128 * tt + 128],
                    WvT_sb[:, 1024 * dk + 512 * half:1024 * dk + 512 * half + 512],
                    start=(dk == 0), stop=(dk == CC - 1)))(),
                  reads=[f"ka_t{tt // 4}", f"wvf{dk}"], writes=[ptok], name=f"vm{tt}_{half}_{dk}")
            A("dve", (lambda tt=tt, half=half, slot=slot: lambda v: v.tensor_mul(
                dscr[:, :], slot[0][:, slot[1]:slot[1] + 512],
                E_r[:, 1024 * tt + 512 * half:1024 * tt + 512 * half + 512]))(),
              reads=[ptok, f"rs_t{tt}"], writes=["p9scr", ptok], name=f"v1{tt}_{half}")
            A("dve", (lambda tt=tt, half=half: lambda v: v.tensor_add(
                Ax[:, 1024 * tt + 512 * half:1024 * tt + 512 * half + 512],
                Ax[:, 1024 * tt + 512 * half:1024 * tt + 512 * half + 512],
                dscr[:, :]))(),
              reads=["p9scr", f"res{tt}"], writes=[f"res{tt}"], name=f"v2{tt}_{half}")
        A("sync", (lambda tt=tt: lambda sy: sy.dma_start(
            out_t[128 * tt:128 * (tt + 1), :], Ax[:, 1024 * tt:1024 * (tt + 1)]))(),
          reads=[f"res{tt}"], writes=[], is_dma=True, name=f"st{tt}", batch="st")
        n_store += 1

    s.finalize()

    qtot = {}
    for op in s.ops:
        if op.is_dma:
            k = s.semkey(op)
            qtot[k] = max(qtot.get(k, 0), op.inc_no)
    import contextlib
    with contextlib.ExitStack() as stack:
        sems = {k: stack.enter_context(nc.semaphore("s_" + k)) for k in s.semkeys}
        blk = stack.enter_context(nc.Block())
        bodies = s.make_bodies(sems)
        blk.tensor(lambda t: bodies["pe"](t))
        blk.scalar(lambda sc: bodies["act"](sc))
        blk.vector(lambda v: bodies["dve"](v))
        blk.gpsimd(lambda g: bodies["sp"](g))

        def syb(sy):
            bodies["sync"](sy)
            for k in sorted(qtot):
                sy.wait_ge(sems[k], qtot[k])
        blk.sync(syb)
    return nc


_PERM = np.concatenate(
    [64 * h + np.concatenate([np.arange(0, 64, 2), np.arange(1, 64, 2)])
     for h in range(H)])


def kernel(L0_attn_x, L0_attn_Wq, L0_attn_Wk, L0_attn_Wv, L0_attn_Wo, L0_attn_bo,
           L0_ffn_W_r, L0_ffn_W_k, L0_ffn_W_v, L0_ffn_mu_r, L0_ffn_mu_k):
    x = np.ascontiguousarray(np.asarray(L0_attn_x, dtype=np.float32))
    wq = np.ascontiguousarray(np.asarray(L0_attn_Wq, dtype=np.float32)[:, _PERM])
    wk = np.ascontiguousarray(np.asarray(L0_attn_Wk, dtype=np.float32)[:, _PERM])
    wv = np.ascontiguousarray(np.asarray(L0_attn_Wv, dtype=np.float32))
    wo = np.ascontiguousarray(np.asarray(L0_attn_Wo, dtype=np.float32))
    bo = np.ascontiguousarray(np.asarray(L0_attn_bo, dtype=np.float32))
    wrt = np.ascontiguousarray(np.asarray(L0_ffn_W_r, dtype=np.float32).T)
    wkt = np.ascontiguousarray(np.asarray(L0_ffn_W_k, dtype=np.float32).T)
    wvt = np.ascontiguousarray(np.asarray(L0_ffn_W_v, dtype=np.float32).T)
    mur = np.ascontiguousarray(np.asarray(L0_ffn_mu_r, dtype=np.float32))
    muk = np.ascontiguousarray(np.asarray(L0_ffn_mu_k, dtype=np.float32))

    nc = _build_nc()
    in_maps = []
    for b in range(8):
        in_maps.append({"x": np.ascontiguousarray(x[b]), "wq": wq, "wk": wk,
                        "wv": wv, "wo": wo, "bo": bo, "wrt": wrt, "wkt": wkt,
                        "wvt": wvt, "mur": mur, "muk": muk})
    res = run_bass_kernel_spmd(nc, in_maps, core_ids=list(range(8)))
    return np.stack([r["out"] for r in res.results], axis=0)


if __name__ == "__main__":
    rng = np.random.default_rng(0)
    ins = {
        "L0_attn_x": rng.standard_normal((8, T, C), dtype=np.float32),
        "L0_attn_Wq": rng.standard_normal((C, C), dtype=np.float32) * 0.02,
        "L0_attn_Wk": rng.standard_normal((C, C), dtype=np.float32) * 0.02,
        "L0_attn_Wv": rng.standard_normal((C, C), dtype=np.float32) * 0.02,
        "L0_attn_Wo": rng.standard_normal((C, C), dtype=np.float32) * 0.02,
        "L0_attn_bo": rng.standard_normal(C, dtype=np.float32) * 0.1,
        "L0_ffn_W_r": rng.standard_normal((C, C), dtype=np.float32) * 0.03,
        "L0_ffn_W_k": rng.standard_normal((C, C), dtype=np.float32) * 0.03,
        "L0_ffn_W_v": rng.standard_normal((C, C), dtype=np.float32) * 0.03,
        "L0_ffn_mu_r": rng.random(C, dtype=np.float32),
        "L0_ffn_mu_k": rng.random(C, dtype=np.float32),
    }
    out = kernel(**ins)
    print("out", out.shape, out.dtype, float(np.abs(out).mean()))
